# version 05
# baseline (speedup 1.0000x reference)
import os
os.environ.setdefault("GAUGE_NTFF_IGNORE_DMA", "1")

import numpy as np
from contextlib import ExitStack
import concourse.bass as bass
import concourse.mybir as mybir
from concourse.bass_utils import run_bass_kernel_spmd

F32 = mybir.dt.float32
BF16 = mybir.dt.bfloat16
AF = mybir.ActivationFunctionType
OP = mybir.AluOpType
AX = mybir.AxisListType

ENGINES = ("tensor", "vector", "scalar", "sync", "gpsimd")
EPS_RMS = 2.220446049250313e-16


class Ch:
    __slots__ = ("h", "n", "name", "owner")

    def __init__(self, h, name, owner=None):
        self.h = h
        self.n = 0
        self.name = name
        self.owner = owner


class B:
    def __init__(self, nc):
        self.nc = nc
        self.streams = {e: [] for e in ENGINES}
        self.stack = ExitStack()
        self.waited = {e: {} for e in ENGINES}
        self.names = set()
        self.engch = {}

    def _nm(self, name):
        i, n = 0, name
        while n in self.names:
            i += 1
            n = f"{name}_{i}"
        self.names.add(n)
        return n

    def ch(self, name, owner=None):
        h = self.stack.enter_context(self.nc.semaphore(self._nm(name)))
        return Ch(h, name, owner)

    def sb(self, name, shape, dtype, stack=None):
        return (stack or self.stack).enter_context(
            self.nc.sbuf_tensor(self._nm(name), shape, dtype))

    def step(self, eng, emit_fn, waits=(), inc=None, inc_n=1):
        wl = []
        ech = self.engch.get(eng)
        if ech is not None and ech.n > 0:
            wl.append((ech.h, ech.n))
        if ech is not None and inc is None:
            inc = ech
        for w in waits:
            if w is None:
                continue
            if isinstance(w, tuple):
                ch, v = w
            else:
                ch, v = w, w.n
            if v <= 0:
                continue
            seen = self.waited[eng].get(ch, -1)
            if v <= seen:
                continue
            self.waited[eng][ch] = v
            wl.append((ch.h, v))
        if inc is not None:
            inc_h = inc.h
            inc.n += inc_n
        else:
            inc_h = None

        def run(E, wl=wl, emit_fn=emit_fn, inc_h=inc_h, inc_n=inc_n):
            for h, v in wl:
                E.wait_ge(h, v)
            last = emit_fn(E)
            if inc_h is not None:
                last.then_inc(inc_h, inc_n)

        self.streams[eng].append(run)

    def dma(self, eng, out, in_, waits=(), inc=None, **kw):
        self.step(eng, lambda E: E.dma_start(out=out, in_=in_, **kw),
                  waits=waits, inc=inc, inc_n=16)

    def finalize(self):
        with self.nc.Block() as blk:
            for eng in ENGINES:
                fns = self.streams[eng]
                if not fns:
                    continue
                getattr(blk, eng)(lambda E, fns=fns: [f(E) for f in fns])
        self.stack.close()


class Rot:
    def __init__(self, bufs, chans=None):
        self.bufs = list(bufs)
        self.guard = [None] * len(self.bufs)
        self.chans = chans
        self.i = 0

    def acquire(self):
        s = self.i % len(self.bufs)
        self.i += 1
        g = self.guard[s]
        self.guard[s] = None
        self._last = s
        ret = self.bufs[s], ([g] if g is not None else [])
        if self.chans is not None:
            return ret + (self.chans[s],)
        return ret

    def release(self, ch):
        self.guard[self._last] = (ch, ch.n)


class Pool:

    def __init__(self, t, nslots, cols, chans=None):
        self.t = t
        self.cols = cols
        self.guard = [None] * nslots
        self.held = [False] * nslots
        self.chans = chans
        self.n = nslots
        self.i = 0

    def acquire(self, k=1):
        for off in range(self.n):
            s = ((self.i + off) % self.n) // k * k
            if s + k <= self.n and not any(self.held[s:s + k]):
                break
        else:
            raise RuntimeError("pool exhausted")
        self.i = (s + k) % self.n
        g = [x for x in self.guard[s:s + k] if x is not None]
        for j in range(s, s + k):
            self.guard[j] = None
            self.held[j] = True
        self._last = (s, k)
        ret = self.t[:, s * self.cols:(s + k) * self.cols], g
        if self.chans is not None:
            return ret + (self.chans[s],)
        return ret

    def release(self, ch, rng=None):
        s, k = self._last if rng is None else rng
        for j in range(s, s + k):
            self.guard[j] = (ch, ch.n)
            self.held[j] = False

    def last(self):
        return self._last


def nchunks(total, mx=512):
    out, off = [], 0
    while off < total:
        c = min(mx, total - off)
        out.append((off, c))
        off += c
    return out


class CFG:
    def __init__(self, D=1024, T=2048, S=4, E0=8, I0=4096, E1=15, H1=1024,
                 n_cores=8, sub=None):
        self.D, self.T, self.S = D, T, S
        self.E0, self.I0, self.E1, self.H1 = E0, I0, E1, H1
        self.KD = D // 128
        self.KI0 = I0 // 128
        self.KH1 = H1 // 128
        self.H = D // 128
        self.TH = T // 2
        self.NTC = T // 128
        self.NTCH = self.TH // 128
        self.n_cores = n_cores
        self.MB = 4
        self.QG = 1
        self.sub = sub if sub is not None else ["f0", "a0", "f0", "f1", "a1", "f1"]


class KB:
    def __init__(self, cfg: CFG):
        self.c = cfg
        self.nc = bass.Bass("TRN2", target_bir_lowering=False)
        self.b = B(self.nc)

    def declare_io(self):
        c, nc = self.c, self.nc
        D = c.D
        self.x_h = nc.dram_tensor("x", [c.S * c.T, D], F32, kind="ExternalInput").ap()
        self.out_h = nc.dram_tensor("out", [c.S * c.T, D], F32, kind="ExternalOutput").ap()
        dram = lambda n, sh: nc.dram_tensor(n, sh, F32, kind="ExternalInput").ap()
        self.w = {
            "gate_w": dram("L0_ffn_gate_w", [D, c.E0]),
            "gate_b": dram("L0_ffn_bias", [c.E0]),
            "up_w": dram("L0_ffn_up_w", [c.E0, D, c.I0]),
            "up_b": dram("L0_ffn_up_b", [c.E0, c.I0]),
            "down_w": dram("L0_ffn_down_w", [c.E0, c.I0, D]),
            "down_b": dram("L0_ffn_down_b", [c.E0, D]),
            "sw_up": dram("L0_ffn_sw_up", [D, c.I0]),
            "sb_up": dram("L0_ffn_sb_up", [c.I0]),
            "sw_down": dram("L0_ffn_sw_down", [c.I0, D]),
            "sb_down": dram("L0_ffn_sb_down", [D]),
            "Wq": dram("L0_attn_Wq", [D, D]),
            "Wk": dram("L0_attn_Wk", [D, D]),
            "Wv": dram("L0_attn_Wv", [D, D]),
            "Wo": dram("L0_attn_Wo", [D, D]),
            "bo": dram("L0_attn_bo", [D]),
            "norm_w": dram("L0_attn_norm_w", [D]),
            "dw": dram("L0_attn_diff_weight", [1, 1]),
            "W_qkva": dram("L1_attn_W_qkva", [3 * D, D]),
            "gamma": dram("L1_attn_gamma", [D]),
            "router_W": dram("L1_ffn_router_W", [D, c.E1]),
            "router_b": dram("L1_ffn_router_b", [c.E1]),
            "sW1": dram("L1_ffn_sW1", [D, c.H1]),
            "sb1": dram("L1_ffn_sb1", [c.H1]),
            "sW2": dram("L1_ffn_sW2", [c.H1, D]),
            "sb2": dram("L1_ffn_sb2", [D]),
            "eW1": dram("L1_ffn_eW1", [c.E1, D, c.H1]),
            "eb1": dram("L1_ffn_eb1", [c.E1, c.H1]),
            "eW2": dram("L1_ffn_eW2", [c.E1, c.H1, D]),
            "eb2": dram("L1_ffn_eb2", [c.E1, D]),
        }

    def alloc(self):
        b, c = self.b, self.c
        self.te = b.ch("te", owner="tensor")
        self.ve = b.ch("ve", owner="vector")
        self.se = b.ch("se", owner="scalar")
        self.ge = b.ch("ge", owner="gpsimd")
        self.dg = b.ch("dg")
        self.ds = b.ch("ds")
        b.engch = {"tensor": self.te, "vector": self.ve, "scalar": self.se,
                   "gpsimd": self.ge}

        D, T = c.D, c.T
        self.R = b.sb("R", [128, c.KD * T], F32)
        self.XN = b.sb("XN", [128, c.KD * T], BF16)
        self.stg = Rot([b.sb(f"stg{i}", [128, D], F32) for i in range(1)],
                       chans=[b.ch(f"sch{i}") for i in range(1)])
        self.wp = Pool(b.sb("wpool", [128, 5 * 4096], BF16), 5, 4096,
                       chans=[b.ch(f"wch{i}") for i in range(5)])
        self.pp = Pool(self.b.stack.enter_context(
            self.nc.psum_tensor("pspool", [128, 8 * 512], F32)), 8, 512)

        self.onesb = b.sb("onesb", [128, 128], BF16)
        self.onesf = b.sb("onesf", [128, 128], F32)
        self.idb = b.sb("idb", [128, 128], BF16)
        self.idf = b.sb("idf", [128, 128], F32)
        self.gateW = b.sb("gateW", [128, c.KD * c.E0], BF16)
        self.routW = b.sb("routW", [128, c.KD * c.E1], BF16)
        self.gateB = b.sb("gateB", [max(c.E0, c.E1), 2], F32)
        self.upB = b.sb("upB", [128, c.E0 * c.KI0], F32)
        self.swUpB = b.sb("swUpB", [128, c.KI0], F32)
        self.sbDs = b.sb("sbDs", [128, c.KD], F32)
        self.dnBr = b.sb("dnBr", [c.E0, D], BF16)
        self.eb1c = b.sb("eb1c", [128, c.E1 * c.KH1], F32)
        self.sb1c = b.sb("sb1c", [128, c.KH1], F32)
        self.sb2r = b.sb("sb2r", [1, D], BF16)
        self.eb2r = b.sb("eb2r", [c.E1, D], BF16)
        self.boS = b.sb("boS", [128, c.KD], F32)
        self.nwc = b.sb("nwc", [128, c.KD], F32)
        self.gmc = b.sb("gmc", [128, c.KD], F32)
        self.sc1 = b.sb("sc1", [128, 1], F32)
        self.dwn = b.sb("dwn", [128, 1], F32)
        self.cv = b.sb("cv", [128, 3], F32)
        self.lnD2 = 0.5 * float(np.log(c.D))
        for i, v in enumerate((1e-5, EPS_RMS, self.lnD2)):
            self.nc.const_aps.register(F32, v, self.cv[:, i:i + 1])

    def prologue(self):
        b, c = self.b, self.c
        te, ve, se, ge, ds = self.te, self.ve, self.se, self.ge, self.ds
        for i, v in enumerate((1e-5, EPS_RMS, self.lnD2)):
            b.step("vector", lambda E, i=i, v=v: E.memset(self.cv[:, i:i + 1], v),
                   inc=ve)
        b.step("vector", lambda E: E.memset(self.onesb[:, :], 1.0), inc=ve)
        b.step("vector", lambda E: E.memset(self.onesf[:, :], 1.0), inc=ve)
        b.step("vector", lambda E: E.memset(self.idb[:, :], 0.0), inc=ve)
        b.step("vector", lambda E: E.memset(self.idf[:, :], 0.0), inc=ve)
        for t in (self.idb, self.idf):
            b.step("gpsimd", lambda E, t=t: E.affine_select(
                out=t[:, :], in_=t[:, :], compare_op=OP.not_equal, fill=1.0,
                base=0, pattern=[[-1, 128]], channel_multiplier=1),
                waits=[ve], inc=ge)

        with ExitStack() as S:
            tmp = b.sb("pro_tmp", [32, 128], F32, stack=S)

            def feat_vec(src_ap, dst, dcol, n):
                kk = n // 128
                ps_, gp = self.pp.acquire()
                b.dma("sync", tmp[:kk, :], src_ap.rearrange("(k p) -> k p", p=128),
                      waits=[(te, te.n), (ds, ds.n)], inc=ds)
                b.step("tensor", lambda E, kk=kk, ps_=ps_: E.transpose(
                    ps_[:, :kk], tmp[:kk, :], self.idf[:kk, :kk]),
                    waits=[ds, ge] + gp, inc=te)
                b.step("vector", lambda E, kk=kk, ps_=ps_, dst=dst, dcol=dcol:
                       E.tensor_copy(out=dst[:, dcol:dcol + kk], in_=ps_[:, :kk]),
                       waits=[te], inc=ve)
                self.pp.release(ve)

            w = self.w
            feat_vec(w["sb_up"], self.swUpB, 0, c.I0)
            for e in range(c.E0):
                feat_vec(w["up_b"][e, :], self.upB, e * c.KI0, c.I0)
            feat_vec(w["sb_down"], self.sbDs, 0, c.D)
            for e in range(c.E1):
                feat_vec(w["eb1"][e, :], self.eb1c, e * c.KH1, c.H1)
            feat_vec(w["sb1"], self.sb1c, 0, c.H1)
            feat_vec(w["bo"], self.boS, 0, c.D)
            feat_vec(w["norm_w"], self.nwc, 0, c.D)
            feat_vec(w["gamma"], self.gmc, 0, c.D)

            b.dma("gpsimd", self.sb2r[0:1, :], w["sb2"][None, :], inc=self.dg)
            b.dma("gpsimd", self.eb2r[:, :], w["eb2"], inc=self.dg)
            b.dma("gpsimd", self.dnBr[:, :], w["down_b"], inc=self.dg)
            b.dma("sync", self.gateB[:c.E0, 0:1], w["gate_b"][:, None], inc=ds,
                  waits=[(ds, ds.n)], allow_slow_non_contiguous=True)
            b.dma("sync", self.gateB[:c.E1, 1:2], w["router_b"][:, None], inc=ds,
                  waits=[(ds, ds.n)], allow_slow_non_contiguous=True)
            b.dma("gpsimd", self.gateW[:, :].rearrange("p (k e) -> p k e", k=c.KD),
                  w["gate_w"].rearrange("(k p) e -> p k e", p=128), inc=self.dg)
            b.dma("gpsimd", self.routW[:, :].rearrange("p (k e) -> p k e", k=c.KD),
                  w["router_W"].rearrange("(k p) e -> p k e", p=128), inc=self.dg)

            dwt = b.sb("pro_dw", [1, 128], F32, stack=S)
            ps_, gp = self.pp.acquire()
            b.dma("sync", dwt[0:1, 0:1], w["dw"], waits=[(ds, ds.n)], inc=ds)
            b.step("vector", lambda E: E.tensor_scalar(
                out=dwt[0:1, :], in0=self.onesf[0:1, :], scalar1=dwt[0:1, 0:1],
                scalar2=None, op0=OP.mult), waits=[ds], inc=ve)
            b.step("tensor", lambda E, ps_=ps_: E.transpose(
                ps_[:, 0:1], dwt[0:1, :], self.idf[0:1, 0:1]),
                waits=[ve, ge] + gp, inc=te)
            b.step("vector", lambda E, ps_=ps_: E.tensor_scalar(
                out=self.sc1[:, :], in0=ps_[:, 0:1], scalar1=-1.0, scalar2=1.0,
                op0=OP.mult, op1=OP.add), waits=[te], inc=ve)
            b.step("vector", lambda E, ps_=ps_: E.tensor_scalar(
                out=self.dwn[:, :], in0=ps_[:, 0:1], scalar1=-1.0, scalar2=None,
                op0=OP.mult), inc=ve)
            self.pp.release(ve)
            b.step("vector", lambda E: E.tensor_scalar(
                out=self.boS[:, :], in0=self.boS[:, :], scalar1=self.sc1[:, 0:1],
                scalar2=None, op0=OP.mult), inc=ve)
            b.step("vector", lambda E: E.tensor_scalar(
                out=self.sbDs[:, :], in0=self.sbDs[:, :], scalar1=0.1, scalar2=None,
                op0=OP.mult), inc=ve)

    def ln(self, S, off, L, r_ready):
        b, c = self.b, self.c
        te, ve, se = self.te, self.ve, self.se
        T = c.T
        nn = nchunks(L)
        S = self._ln_stack = ExitStack()
        sq = Rot([b.sb(f"ln_sq{i}", [128, L], BF16, stack=S) for i in range(2)])
        xb = Rot([b.sb(f"ln_xb{i}", [128, L], BF16, stack=S) for i in range(2)])
        mb = b.sb("ln_m", [128, L], F32, stack=S)
        ab = b.sb("ln_a", [128, L], F32, stack=S)
        tb = b.sb("ln_t", [128, L], F32, stack=S)
        nsl = max(1, L // 512)
        pss, g1 = self.pp.acquire(nsl)
        rs1 = self.pp.last()
        psq, g2 = self.pp.acquire(nsl)
        rs2 = self.pp.last()
        for k in range(c.KD):
            a0 = k * T + off
            sqb, gq = sq.acquire()
            b.step("scalar", lambda E, a0=a0, sqb=sqb: E.activation(
                out=sqb[:, :], in_=self.R[:, a0:a0 + L], func=AF.Square),
                waits=[r_ready] + gq, inc=se)
            xbb, gx = xb.acquire()
            b.step("vector", lambda E, a0=a0, xbb=xbb: E.tensor_copy(
                out=xbb[:, :], in_=self.R[:, a0:a0 + L]), waits=[r_ready] + gx,
                inc=ve)
            b.step("tensor", lambda E, k=k, xbb=xbb: [E.matmul(
                pss[:, o:o + n], self.onesb[:, :], xbb[:, o:o + n],
                start=(k == 0), stop=(k == c.KD - 1)) for o, n in nn][-1],
                waits=[ve] + (g1 if k == 0 else []), inc=te)
            xb.release(te)
            b.step("tensor", lambda E, k=k, sqb=sqb: [E.matmul(
                psq[:, o:o + n], self.onesb[:, :], sqb[:, o:o + n],
                start=(k == 0), stop=(k == c.KD - 1)) for o, n in nn][-1],
                waits=[se] + (g2 if k == 0 else []), inc=te)
            sq.release(te)
        rD = 1.0 / c.D
        b.step("vector", lambda E: E.tensor_scalar(
            out=mb[:, :], in0=pss[:, :L], scalar1=rD, scalar2=None, op0=OP.mult),
            waits=[te], inc=ve)
        b.step("vector", lambda E: E.tensor_mul(tb[:, :], mb[:, :], mb[:, :]), inc=ve)
        b.step("vector", lambda E: E.scalar_tensor_tensor(
            out=tb[:, :], in0=psq[:, :L], scalar=rD, in1=tb[:, :],
            op0=OP.mult, op1=OP.subtract), inc=ve)
        self.pp.release(ve, rng=rs1)
        self.pp.release(ve, rng=rs2)
        b.step("scalar", lambda E: E.activation(
            out=ab[:, :], in_=tb[:, :], func=AF.Ln, bias=1e-5), waits=[ve], inc=se)
        b.step("scalar", lambda E: E.activation(
            out=ab[:, :], in_=ab[:, :], func=AF.Exp, scale=-0.5), inc=se)
        for k in range(c.KD):
            a0 = k * T + off
            b.step("vector", lambda E, a0=a0: E.tensor_sub(
                tb[:, :], self.R[:, a0:a0 + L], mb[:, :]), waits=[se], inc=ve)
            b.step("vector", lambda E, a0=a0: E.tensor_mul(
                self.XN[:, a0:a0 + L], tb[:, :], ab[:, :]), inc=ve)
        S.close()
        return (ve, ve.n)

    def topk_weights(self, S, xn_ready, off, L, which):
        b, c = self.b, self.c
        te, ve, se = self.te, self.ve, self.se
        E = c.E0 if which == 0 else c.E1
        Wt = self.gateW if which == 0 else self.routW
        NT = L // 128
        nn = nchunks(L)
        w_tm = b.sb("wtm", [128, NT * E], F32, stack=S)
        w_fm = b.sb("wfm", [E, L], BF16, stack=S)
        S2 = ExitStack()
        sco = b.sb("sco", [E, L], F32, stack=S2)
        prb = b.sb("prb", [128, E], F32, stack=S2)
        m1 = b.sb("gm", [128, 6], F32, stack=S2)
        msk = b.sb("gmsk", [128, E], F32, stack=S2)
        wv = b.sb("gwv", [128, E], F32, stack=S2)

        nsl = max(1, L // 512)
        psg, gg = self.pp.acquire(nsl)
        T = c.T
        b.step("tensor", lambda E_: [E_.matmul(
            psg[:E, o:o + n], Wt[:, k * E:(k + 1) * E],
            self.XN[:, k * T + off + o: k * T + off + o + n],
            start=(k == 0), stop=(k == c.KD - 1))
            for o, n in nn for k in range(c.KD)][-1],
            waits=[xn_ready, (self.dg, 80)] + gg, inc=te)
        bias = self.gateB[:E, 0:1] if which == 0 else self.gateB[:E, 1:2]
        if which == 0:
            b.step("scalar", lambda E_: E_.activation(
                out=sco[:, :], in_=psg[:E, :L], func=AF.Sigmoid, bias=bias),
                waits=[te], inc=se)
        else:
            b.step("scalar", lambda E_: E_.activation(
                out=sco[:, :], in_=psg[:E, :L], func=AF.Identity, bias=bias),
                waits=[te], inc=se)
        self.pp.release(se)

        for tc in range(NT):
            pst, gt = self.pp.acquire()
            b.step("tensor", lambda E_, tc=tc, pst=pst: E_.transpose(
                pst[:, :E], sco[:, tc * 128:(tc + 1) * 128], self.idf[:E, :E]),
                waits=[se] + gt, inc=te)
            s_ap = pst[:, :E]
            if which == 1:
                b.step("vector", lambda E_, s_ap=s_ap: E_.tensor_reduce(
                    out=m1[:, 0:1], in_=s_ap, axis=AX.X, op=OP.max, negate=True),
                    waits=[te], inc=ve)
                b.step("scalar", lambda E_, s_ap=s_ap: E_.activation(
                    out=prb[:, :E], in_=s_ap, func=AF.Exp, bias=m1[:, 0:1],
                    accum_out=m1[:, 1:2]), waits=[ve], inc=se)
                b.step("vector", lambda E_: E_.reciprocal(
                    out=m1[:, 2:3], in_=m1[:, 1:2]), waits=[se], inc=ve)
                b.step("vector", lambda E_: E_.tensor_scalar(
                    out=prb[:, :E], in0=prb[:, :E], scalar1=m1[:, 2:3],
                    scalar2=None, op0=OP.mult), inc=ve)
                src = prb[:, :E]
                kk = 3
            else:
                b.step("vector", lambda E_, s_ap=s_ap: E_.tensor_copy(
                    out=prb[:, :E], in_=s_ap), waits=[te], inc=ve)
                src = prb[:, :E]
                kk = 2
            self.pp.release(ve)
            b.step("vector", lambda E_: E_.tensor_copy(out=wv[:, :E], in_=src),
                   inc=ve)
            for j in range(kk):
                b.step("vector", lambda E_, j=j: E_.tensor_reduce(
                    out=m1[:, 3 + j:4 + j], in_=wv[:, :E], axis=AX.X, op=OP.max),
                    inc=ve)
                if j < kk - 1:
                    b.step("vector", lambda E_, j=j: E_.tensor_scalar(
                        out=msk[:, :E], in0=wv[:, :E], scalar1=m1[:, 3 + j:4 + j],
                        scalar2=None, op0=OP.is_lt), inc=ve)
                    b.step("vector", lambda E_: E_.tensor_mul(
                        wv[:, :E], msk[:, :E], wv[:, :E]), inc=ve)
            b.step("vector", lambda E_, kk=kk: E_.tensor_scalar(
                out=msk[:, :E], in0=src, scalar1=m1[:, 2 + kk:3 + kk],
                scalar2=None, op0=OP.is_ge), inc=ve)
            dst = w_tm[:, tc * E:(tc + 1) * E]
            b.step("vector", lambda E_, dst=dst: E_.tensor_mul(
                dst, msk[:, :E], src), inc=ve)
            if which == 0:
                b.step("vector", lambda E_: E_.tensor_add(
                    m1[:, 5:6], m1[:, 3:4], m1[:, 4:5]), inc=ve)
                b.step("vector", lambda E_: E_.tensor_scalar(
                    out=m1[:, 5:6], in0=m1[:, 5:6], scalar1=1e-6, scalar2=None,
                    op0=OP.add), inc=ve)
                b.step("vector", lambda E_: E_.reciprocal(
                    out=m1[:, 5:6], in_=m1[:, 5:6]), inc=ve)
                b.step("vector", lambda E_, dst=dst: E_.tensor_scalar(
                    out=dst, in0=dst, scalar1=m1[:, 5:6], scalar2=None,
                    op0=OP.mult), inc=ve)
            psb, gb = self.pp.acquire()
            b.step("tensor", lambda E_, dst=dst, psb=psb: E_.transpose(
                psb[:E, :128], dst, self.idf[:, :]), waits=[ve] + gb, inc=te)
            b.step("vector", lambda E_, tc=tc, psb=psb: E_.tensor_copy(
                out=w_fm[:, tc * 128:(tc + 1) * 128], in_=psb[:E, :128]),
                waits=[te], inc=ve)
            self.pp.release(ve)
        S2.close()
        return w_tm, w_fm, (ve, ve.n)

    def l0ffn(self, r_ready):
        b, c = self.b, self.c
        te, ve, se, dg = self.te, self.ve, self.se, self.dg
        T, TH, D = c.T, c.TH, c.D
        w = self.w

        KIH = c.KI0 // 4
        SB = ExitStack()
        hact = acc = be = None

        def _half(half, r_ready, prep, preps_done):
            off = half * TH
            nn = nchunks(TH)
            first_mm = [preps_done]
            with ExitStack() as S:
                xn_ready, w_tm, w_fm, w_ready = prep
                tmp = Rot([self.stg.bufs[0][:, :TH]])
                units = [("sh", w["sw_up"], w["sw_down"])] + [
                    ("ex", w["up_w"][e], w["down_w"][e]) for e in range(c.E0)]
                for ui, (kind, wu, wd) in enumerate(units):
                    e = ui - 1
                    if kind == "ex":
                        for tc in range(c.NTCH):
                            psb, gb = self.pp.acquire()
                            b.step("vector", lambda E_, tc=tc, e=e: E_.tensor_scalar(
                                out=tmp.bufs[0][:, :128], in0=self.onesf[:, :],
                                scalar1=w_tm[:, tc * c.E0 + e:tc * c.E0 + e + 1],
                                scalar2=0.1, op0=OP.mult, op1=OP.mult),
                                waits=[w_ready, (te, te.n)], inc=ve)
                            b.step("tensor", lambda E_, psb=psb: E_.transpose(
                                psb[:, :128], tmp.bufs[0][:, :128], self.idf[:, :]),
                                waits=[ve] + gb, inc=te)
                            b.step("vector", lambda E_, tc=tc, psb=psb: E_.tensor_copy(
                                out=be[:, tc * 128:(tc + 1) * 128], in_=psb[:, :128]),
                                waits=[te], inc=ve)
                            self.pp.release(ve)
                    for ih in range(c.KI0 // KIH):
                        for mb0 in range(0, KIH, c.MB):
                            mbn = min(c.MB, KIH - mb0)
                            gm0 = ih * KIH + mb0
                            slot, gw, wch = self.wp.acquire()
                            b.dma("gpsimd",
                                  slot[:, :c.KD * mbn * 128].rearrange(
                                      "p (k m) -> p k m", k=c.KD),
                                  wu[:, gm0 * 128:(gm0 + mbn) * 128].rearrange(
                                      "(k p) m -> p k m", p=128),
                                  waits=gw, inc=wch)
                            for mm in range(mbn):
                                m = mb0 + mm
                                gm = gm0 + mm
                                psu, gu = self.pp.acquire(max(1, TH // 512))
                                b.step("tensor", lambda E_, slot=slot, mm=mm,
                                       mbn=mbn, psu=psu: [E_.matmul(
                                           psu[:, o:o + n],
                                           slot[:, (k * mbn + mm) * 128:
                                                (k * mbn + mm) * 128 + 128],
                                           self.XN[:, k * T + off + o:
                                                   k * T + off + o + n],
                                           start=(k == 0), stop=(k == c.KD - 1))
                                           for o, n in nn for k in range(c.KD)][-1],
                                       waits=[wch, xn_ready] + gu + first_mm,
                                       inc=te)
                                bia = (self.swUpB[:, gm:gm + 1] if kind == "sh"
                                       else self.upB[:, e * c.KI0 + gm:
                                                     e * c.KI0 + gm + 1])
                                b.step("scalar", lambda E_, m=m, psu=psu, bia=bia:
                                       E_.activation(
                                           out=hact[:, m * TH:(m + 1) * TH],
                                           in_=psu[:, :TH], func=AF.Gelu, bias=bia),
                                       waits=[te], inc=se)
                                self.pp.release(se)
                            self.wp.release(te)
                        for d in range(c.KD):
                            slot, gw, wch = self.wp.acquire()
                            for rc in range(0, KIH, 8):
                                rn = min(8, KIH - rc)
                                gr0 = (ih * KIH + rc) * 128
                                b.dma("gpsimd",
                                      slot[:, rc * 128:(rc + rn) * 128].rearrange(
                                          "p (k m) -> p k m", k=rn),
                                      wd[gr0:gr0 + rn * 128,
                                         d * 128:(d + 1) * 128].rearrange(
                                          "(k p) m -> p k m", p=128),
                                      waits=gw if rc == 0 else [], inc=wch)
                            psd, gd = self.pp.acquire(max(1, TH // 512))
                            last_half = ih == 1
                            def dn_mm(E_, slot=slot, psd=psd, kind=kind, d=d,
                                      last_half=last_half):
                                last = None
                                for o, n in nn:
                                    for k in range(KIH):
                                        last = E_.matmul(
                                            psd[:, o:o + n],
                                            slot[:, k * 128:(k + 1) * 128],
                                            hact[:, k * TH + o: k * TH + o + n],
                                            start=(k == 0),
                                            stop=(k == KIH - 1 and
                                                  not (kind == "sh" and last_half)))
                                    if kind == "sh" and last_half:
                                        last = E_.matmul(
                                            psd[:, o:o + n],
                                            self.dnBr[:c.E0, d * 128:(d + 1) * 128],
                                            w_fm[:, o:o + n], start=False, stop=True)
                                return last
                            b.step("tensor", dn_mm,
                                   waits=[wch, se, w_ready, (self.dg, 80)] + gd,
                                   inc=te)
                            self.wp.release(te)
                            if kind == "sh" and ih == 0:
                                b.step("scalar", lambda E_, d=d, psd=psd:
                                       E_.activation(
                                           out=acc[:, d * TH:(d + 1) * TH],
                                           in_=psd[:, :TH], func=AF.Identity,
                                           scale=0.1, bias=self.sbDs[:, d:d + 1]),
                                       waits=[te, (ve, ve.n)], inc=se)
                                self.pp.release(se)
                            elif kind == "sh":
                                b.step("vector", lambda E_, d=d, psd=psd:
                                       E_.scalar_tensor_tensor(
                                           out=acc[:, d * TH:(d + 1) * TH],
                                           in0=psd[:, :TH], scalar=0.1,
                                           in1=acc[:, d * TH:(d + 1) * TH],
                                           op0=OP.mult, op1=OP.add),
                                       waits=[te, se], inc=ve)
                                self.pp.release(ve)
                            else:
                                tb, gt = tmp.acquire()
                                b.step("vector", lambda E_, psd=psd, tb=tb:
                                       E_.tensor_mul(tb[:, :], psd[:, :TH],
                                                     be[:, :]),
                                       waits=[te, se] + gt, inc=ve)
                                b.step("vector", lambda E_, d=d, tb=tb:
                                       E_.tensor_add(
                                           acc[:, d * TH:(d + 1) * TH],
                                           acc[:, d * TH:(d + 1) * TH], tb[:, :]),
                                       inc=ve)
                                tmp.release(ve)
                                self.pp.release(ve)
                    dn_read = (te, te.n)
                mxs = b.sb("l0mxs", [128, c.NTCH], F32, stack=S)
                bn = be
                for tc in range(c.NTCH):
                    pst, gt = self.pp.acquire(max(1, c.KD // 4))
                    b.step("tensor", lambda E_, tc=tc, pst=pst: [E_.transpose(
                        pst[:, k * 128:(k + 1) * 128],
                        acc[:, k * TH + tc * 128: k * TH + (tc + 1) * 128],
                        self.idf[:, :]) for k in range(c.KD)][-1],
                        waits=[ve, se] + gt, inc=te)
                    b.step("vector", lambda E_, tc=tc, pst=pst: E_.tensor_reduce(
                        out=mxs[:, tc:tc + 1], in_=pst[:, :c.KD * 128], axis=AX.X,
                        op=OP.max, apply_absolute_value=True),
                        waits=[te], inc=ve)
                    self.pp.release(ve)
                b.step("vector", lambda E_: E_.tensor_scalar(
                    out=mxs[:, :], in0=mxs[:, :], scalar1=1e-6, scalar2=None,
                    op0=OP.add), inc=ve)
                b.step("vector", lambda E_: E_.reciprocal(
                    out=mxs[:, :], in_=mxs[:, :]), inc=ve)
                for tc in range(c.NTCH):
                    psb, gb = self.pp.acquire()
                    b.step("vector", lambda E_, tc=tc: E_.tensor_scalar(
                        out=tmp.bufs[0][:, :128], in0=self.onesf[:, :],
                        scalar1=mxs[:, tc:tc + 1], scalar2=None, op0=OP.mult),
                        waits=[(te, te.n)], inc=ve)
                    b.step("tensor", lambda E_, psb=psb: E_.transpose(
                        psb[:, :128], tmp.bufs[0][:, :128], self.idf[:, :]),
                        waits=[ve] + gb, inc=te)
                    b.step("vector", lambda E_, tc=tc, psb=psb: E_.tensor_copy(
                        out=bn[:, tc * 128:(tc + 1) * 128], in_=psb[:, :128]),
                        waits=[te], inc=ve)
                    self.pp.release(ve)
                for k in range(c.KD):
                    tb, gt = tmp.acquire()
                    b.step("vector", lambda E_, k=k, tb=tb: E_.tensor_mul(
                        tb[:, :], acc[:, k * TH:(k + 1) * TH], bn[:, :]),
                        waits=gt, inc=ve)
                    b.step("vector", lambda E_, k=k, tb=tb: E_.scalar_tensor_tensor(
                        out=self.R[:, k * T + off: k * T + off + TH], in0=tb[:, :],
                        scalar=0.5, in1=self.R[:, k * T + off: k * T + off + TH],
                        op0=OP.mult, op1=OP.add), inc=ve)
                    tmp.release(ve)
            return (ve, ve.n)

        with ExitStack() as SP:
            preps = []
            rr = r_ready
            for half in range(2):
                xn_ready = self.ln(SP, half * TH, TH, rr)
                w_tm, w_fm, w_ready = self.topk_weights(
                    SP, xn_ready, half * TH, TH, 0)
                preps.append((xn_ready, w_tm, w_fm, w_ready))
                rr = (self.ve, self.ve.n)
            hact = b.sb("hact", [128, KIH * TH], BF16, stack=SB)
            acc = b.sb("acc", [128, c.KD * TH], F32, stack=SB)
            be = b.sb("be", [128, TH], F32, stack=SB)
            pd = (self.ve, self.ve.n)
            for half in range(2):
                r_ready = _half(half, r_ready, preps[half], pd)
            SB.close()
        return r_ready

    def l0attn(self, r_ready):
        b, c = self.b, self.c
        te, ve, se, dg = self.te, self.ve, self.se, self.dg
        T, D, F = c.T, c.D, 64
        w = self.w
        scale = float(F) ** -0.5
        with ExitStack() as S:
            xn_ready = self.ln(S, 0, T, r_ready)
            o_fm = b.sb("o_fm", [128, c.KD * T], BF16, stack=S)
            SH = ExitStack()
            qh = b.sb("qh", [128, T], BF16, stack=SH)
            kh = b.sb("kh", [128, T], BF16, stack=SH)
            vh = b.sb("vh", [128, T], BF16, stack=SH)
            p0 = b.sb("p0", [128, T], BF16, stack=SH)
            att = b.sb("att", [128, T], BF16, stack=SH)
            attT = b.sb("attT", [128, c.NTC * c.QG * 128], BF16, stack=SH)
            sc = b.sb("attsc", [128, 8], F32, stack=SH)
            ps_s, gs = self.pp.acquire(max(1, T // 512))
            rs_s = self.pp.last()
            exp_done = (se, se.n)
            nnT = nchunks(T)
            for h in range(c.H):
                raws, gr, rch = self.wp.acquire()
                raws_rng = self.wp.last()
                for j, wm in enumerate((w["Wq"], w["Wk"], w["Wv"])):
                    b.dma("gpsimd", raws[:, j * D:(j + 1) * D],
                          wm[h * 128:(h + 1) * 128, :], waits=gr if j == 0 else [],
                          inc=rch)
                wtb, gw, _wc = self.wp.acquire()
                wtb_rng = self.wp.last()
                for j in range(3):
                    for k in range(c.KD):
                        pst, gp = self.pp.acquire()
                        b.step("tensor", lambda E_, j=j, k=k, pst=pst, raws=raws:
                               E_.transpose(pst[:, :64].bitcast(BF16),
                                            raws[:, j * D + k * 128: j * D + (k + 1) * 128],
                                            self.idb[:, :]),
                               waits=[rch] + gp,
                               inc=te)
                        b.step("vector", lambda E_, j=j, k=k, pst=pst, wtb=wtb:
                               E_.tensor_copy(
                                   out=wtb[:, (j * c.KD + k) * 128:(j * c.KD + k + 1) * 128],
                                   in_=pst[:, :64].bitcast(BF16)),
                               waits=[te] + (gw if j == 0 and k == 0 else []),
                               inc=ve)
                        self.pp.release(ve)
                self.wp.release(te, rng=raws_rng)
                for dst, j in ((qh, 0), (kh, 1)):
                    for o, n in nnT:
                        psq, gp = self.pp.acquire()
                        b.step("tensor", lambda E_, dst=dst, j=j, o=o, n=n, psq=psq,
                               wtb=wtb: [E_.matmul(
                                   psq[:, :n],
                                   wtb[:, (j * c.KD + k) * 128:(j * c.KD + k + 1) * 128],
                                   self.XN[:, k * T + o: k * T + o + n],
                                   start=(k == 0), stop=(k == c.KD - 1))
                                   for k in range(c.KD)][-1],
                               waits=[ve, xn_ready] + gp, inc=te)
                        b.step("scalar", lambda E_, dst=dst, o=o, n=n, psq=psq:
                               E_.activation(out=dst[:, o:o + n], in_=psq[:, :n],
                                             func=AF.Identity), waits=[te], inc=se)
                        self.pp.release(se)
                for tc4 in range(0, c.NTC, 4):
                    t4 = min(4, c.NTC - tc4)
                    psv, gp = self.pp.acquire()
                    b.step("tensor", lambda E_, tc4=tc4, t4=t4, psv=psv, wtb=wtb:
                           [E_.matmul(
                               psv[:, j * 128:(j + 1) * 128],
                               self.XN[:, k * T + (tc4 + j) * 128:
                                       k * T + (tc4 + j + 1) * 128],
                               wtb[:, (2 * c.KD + k) * 128:(2 * c.KD + k + 1) * 128],
                               start=(k == 0), stop=(k == c.KD - 1))
                               for j in range(t4) for k in range(c.KD)][-1],
                           waits=[ve, xn_ready] + gp, inc=te)
                    b.step("scalar", lambda E_, tc4=tc4, t4=t4, psv=psv: E_.activation(
                        out=vh[:, tc4 * 128:(tc4 + t4) * 128], in_=psv[:, :t4 * 128],
                        func=AF.Identity), waits=[te], inc=se)
                    self.pp.release(se)
                self.wp.release(te, rng=wtb_rng)
                for qc in range(c.NTC):
                    qg_i = qc % c.QG
                    b.step("tensor", lambda E_, qc=qc: [E_.matmul(
                        ps_s[:, o:o + n], qh[0:F, qc * 128:(qc + 1) * 128],
                        kh[0:F, o:o + n], start=True, stop=True)
                        for o, n in nnT][-1],
                        waits=[se, exp_done] + (gs if h == 0 and qc == 0 else []),
                        inc=te)
                    b.step("vector", lambda E_: E_.tensor_reduce(
                        out=sc[:, 0:1], in_=ps_s[:, :T], axis=AX.X, op=OP.max,
                        negate=True), waits=[te], inc=ve)
                    b.step("vector", lambda E_: E_.tensor_scalar(
                        out=sc[:, 1:2], in0=sc[:, 0:1], scalar1=scale, scalar2=None,
                        op0=OP.mult), inc=ve)
                    b.step("scalar", lambda E_: E_.activation(
                        out=p0[:, :], in_=ps_s[:, :T], func=AF.Exp, scale=scale,
                        bias=sc[:, 1:2], accum_out=sc[:, 2:3]), waits=[ve], inc=se)
                    b.step("tensor", lambda E_, qc=qc: [E_.matmul(
                        ps_s[:, o:o + n], qh[F:128, qc * 128:(qc + 1) * 128],
                        kh[F:128, o:o + n], start=True, stop=True)
                        for o, n in nnT][-1], waits=[se], inc=te)
                    b.step("vector", lambda E_: E_.tensor_reduce(
                        out=sc[:, 3:4], in_=ps_s[:, :T], axis=AX.X, op=OP.max,
                        negate=True), waits=[te], inc=ve)
                    b.step("vector", lambda E_: E_.tensor_scalar(
                        out=sc[:, 4:5], in0=sc[:, 3:4], scalar1=scale, scalar2=None,
                        op0=OP.mult), inc=ve)
                    b.step("scalar", lambda E_: E_.activation(
                        out=att[:, :], in_=ps_s[:, :T], func=AF.Exp, scale=scale,
                        bias=sc[:, 4:5], accum_out=sc[:, 5:6]), waits=[ve], inc=se)
                    exp_done = (se, se.n)
                    b.step("vector", lambda E_: E_.reciprocal(
                        out=sc[:, 6:7], in_=sc[:, 5:6]), waits=[se], inc=ve)
                    b.step("vector", lambda E_: E_.tensor_mul(
                        sc[:, 6:7], sc[:, 6:7], sc[:, 2:3]), inc=ve)
                    b.step("vector", lambda E_: E_.tensor_mul(
                        sc[:, 6:7], sc[:, 6:7], self.dwn[:, 0:1]), inc=ve)
                    b.step("vector", lambda E_: E_.reciprocal(
                        out=sc[:, 7:8], in_=sc[:, 2:3]), inc=ve)
                    b.step("vector", lambda E_: E_.scalar_tensor_tensor(
                        out=att[:, :], in0=att[:, :], scalar=sc[:, 6:7],
                        in1=p0[:, :], op0=OP.mult, op1=OP.add), inc=ve)
                    b.step("vector", lambda E_: E_.tensor_scalar(
                        out=att[:, :], in0=att[:, :], scalar1=sc[:, 7:8],
                        scalar2=None, op0=OP.mult), inc=ve)
                    for kc4 in range(0, c.NTC, 4):
                        k4 = min(4, c.NTC - kc4)
                        pst, gp = self.pp.acquire()
                        b.step("tensor", lambda E_, kc4=kc4, k4=k4, pst=pst: [
                            E_.transpose(pst[:, j * 64:(j + 1) * 64].bitcast(BF16),
                                         att[:, (kc4 + j) * 128:(kc4 + j + 1) * 128],
                                         self.idb[:, :]) for j in range(k4)][-1],
                            waits=[ve] + gp, inc=te)
                        dst = attT[:, :].rearrange(
                            "p (k q) -> p k q", q=c.QG * 128)[
                            :, kc4:kc4 + k4, qg_i * 128:(qg_i + 1) * 128]
                        b.step("vector", lambda E_, pst=pst, dst=dst, k4=k4:
                               E_.tensor_copy(out=dst, in_=pst[:, :k4 * 64]
                                              .bitcast(BF16)
                                              .rearrange("p (k q) -> p k q", k=k4)),
                               waits=[te], inc=ve)
                        self.pp.release(ve)
                    if qg_i == c.QG - 1:
                        qg0 = qc - c.QG + 1
                        pso, gp = self.pp.acquire()
                        b.step("tensor", lambda E_, pso=pso: [E_.matmul(
                            pso[:, :c.QG * 128],
                            vh[:, kc * 128:(kc + 1) * 128],
                            attT[:, kc * c.QG * 128:(kc + 1) * c.QG * 128],
                            start=(kc == 0), stop=(kc == c.NTC - 1))
                            for kc in range(c.NTC)][-1], waits=[ve] + gp, inc=te)
                        b.step("scalar", lambda E_, h=h, qg0=qg0, pso=pso:
                               E_.activation(
                                   out=o_fm[:, h * T + qg0 * 128:
                                            h * T + (qg0 + c.QG) * 128],
                                   in_=pso[:, :c.QG * 128], func=AF.Identity),
                               waits=[te], inc=se)
                        self.pp.release(se)
            self.pp.release(se, rng=rs_s)
            SH.close()
            osq = Rot([b.sb(f"osq{i}", [128, T], BF16, stack=S) for i in range(2)])
            rmsb = b.sb("rmsb", [128, T], F32, stack=S)
            ps_st, gst = self.pp.acquire(max(1, T // 512))
            for k in range(c.KD):
                sq_, gq = osq.acquire()
                b.step("scalar", lambda E_, k=k, sq_=sq_: E_.activation(
                    out=sq_[:, :], in_=o_fm[:, k * T:(k + 1) * T], func=AF.Square),
                    waits=[se] + gq, inc=se)
                b.step("tensor", lambda E_, k=k, sq_=sq_: [E_.matmul(
                    ps_st[:, o:o + n], self.onesb[:, :], sq_[:, o:o + n],
                    start=(k == 0), stop=(k == c.KD - 1)) for o, n in nnT][-1],
                    waits=[se] + (gst if k == 0 else []), inc=te)
                osq.release(te)
            b.step("scalar", lambda E_: E_.activation(
                out=rmsb[:, :], in_=ps_st[:, :T], func=AF.Ln, scale=1.0 / c.D,
                bias=EPS_RMS), waits=[te], inc=se)
            self.pp.release(se)
            b.step("scalar", lambda E_: E_.activation(
                out=rmsb[:, :], in_=rmsb[:, :], func=AF.Exp, scale=-0.5), inc=se)
            for k in range(c.KD):
                b.step("vector", lambda E_, k=k: E_.scalar_tensor_tensor(
                    out=o_fm[:, k * T:(k + 1) * T], in0=o_fm[:, k * T:(k + 1) * T],
                    scalar=self.nwc[:, k:k + 1], in1=rmsb[:, :], op0=OP.mult,
                    op1=OP.mult), waits=[se], inc=ve)
            xw_ready = (ve, ve.n)
            for d in range(c.KD):
                raws, gr, rch = self.wp.acquire()
                raws_rng = self.wp.last()
                b.dma("gpsimd", raws[:, :D], w["Wo"][d * 128:(d + 1) * 128, :],
                      waits=gr, inc=rch)
                wtb, gw0, _wc = self.wp.acquire()
                wtb_rng2 = self.wp.last()
                for k in range(c.KD):
                    pst, gp = self.pp.acquire()
                    b.step("tensor", lambda E_, k=k, pst=pst, raws=raws: E_.transpose(
                        pst[:, :64].bitcast(BF16), raws[:, k * 128:(k + 1) * 128],
                        self.idb[:, :]),
                        waits=[rch] + gp, inc=te)
                    b.step("vector", lambda E_, k=k, pst=pst, wtb=wtb: E_.tensor_copy(
                        out=wtb[:, k * 128:(k + 1) * 128],
                        in_=pst[:, :64].bitcast(BF16)),
                        waits=[te] + (gw0 if k == 0 else []), inc=ve)
                    self.pp.release(ve)
                self.wp.release(te, rng=raws_rng)
                for o, n in nnT:
                    pso, gp = self.pp.acquire()
                    b.step("tensor", lambda E_, d=d, o=o, n=n, pso=pso, wtb=wtb:
                           [E_.matmul(pso[:, :n],
                                      wtb[:, k * 128:(k + 1) * 128],
                                      o_fm[:, k * T + o: k * T + o + n],
                                      start=(k == 0), stop=(k == c.KD - 1))
                               for k in range(c.KD)][-1],
                           waits=[ve, xw_ready] + gp, inc=te)
                    b.step("scalar", lambda E_, pso=pso, d=d, o=o, n=n: E_.activation(
                        out=rmsb[:, o:o + n], in_=pso[:, :n], func=AF.Identity,
                        scale=self.sc1[:, 0:1], bias=self.boS[:, d:d + 1]),
                        waits=[te], inc=se)
                    b.step("vector", lambda E_, d=d, o=o, n=n: E_.tensor_add(
                        self.R[:, d * T + o: d * T + o + n],
                        self.R[:, d * T + o: d * T + o + n], rmsb[:, o:o + n]),
                        waits=[se], inc=ve)
                    self.pp.release(se)
                self.wp.release(te, rng=wtb_rng2)
        return (ve, ve.n)

    def l1ffn(self, r_ready):
        b, c = self.b, self.c
        te, ve, se, dg = self.te, self.ve, self.se, self.dg
        T, TH, D, H1 = c.T, c.TH, c.D, c.H1
        w = self.w
        nnD = nchunks(D)

        SB = ExitStack()
        hact = acc = None

        def _half(half, r_ready, prep, preps_done):
            off = half * TH
            nn = nchunks(TH)
            first_mm = [preps_done]
            with ExitStack() as S:
                xn_ready, cw_tm, cw_fm, w_ready = prep
                units = [("sh", w["sW1"], w["sW2"])] + [
                    ("ex", w["eW1"][e], w["eW2"][e]) for e in range(c.E1)]
                for ui, (kind, w1, w2) in enumerate(units):
                    e = ui - 1
                    w2s, gw2, w2ch = self.wp.acquire(2)
                    rw2 = self.wp.last()
                    for rc in range(0, c.KH1, 4):
                        rn = min(4, c.KH1 - rc)
                        b.dma("gpsimd",
                              w2s[:, rc * D:(rc + rn) * D].rearrange(
                                  "p (k m) -> p k m", k=rn),
                              w2[rc * 128:(rc + rn) * 128, :].rearrange(
                                  "(k p) m -> p k m", p=128),
                              waits=gw2 if rc == 0 else [], inc=w2ch)
                    for mb0 in range(0, c.KH1, c.MB):
                        mbn = min(c.MB, c.KH1 - mb0)
                        slot, gw, wch = self.wp.acquire()
                        b.dma("gpsimd",
                              slot[:, :c.KD * mbn * 128].rearrange(
                                  "p (k m) -> p k m", k=c.KD),
                              w1[:, mb0 * 128:(mb0 + mbn) * 128].rearrange(
                                  "(k p) m -> p k m", p=128),
                              waits=gw, inc=wch)
                        for mm in range(mbn):
                            m = mb0 + mm
                            psu, gu = self.pp.acquire(max(1, TH // 512))
                            b.step("tensor", lambda E_, slot=slot, mm=mm, mbn=mbn,
                                   psu=psu: [E_.matmul(
                                       psu[:, o:o + n],
                                       slot[:, (k * mbn + mm) * 128:
                                            (k * mbn + mm) * 128 + 128],
                                       self.XN[:, k * T + off + o:
                                               k * T + off + o + n],
                                       start=(k == 0), stop=(k == c.KD - 1))
                                       for o, n in nn for k in range(c.KD)][-1],
                                   waits=[wch, xn_ready] + gu + first_mm, inc=te)
                            bia = (self.sb1c[:, m:m + 1] if kind == "sh"
                                   else self.eb1c[:, e * c.KH1 + m:e * c.KH1 + m + 1])
                            b.step("scalar", lambda E_, m=m, psu=psu, bia=bia:
                                   E_.activation(
                                       out=hact[:, m * TH:(m + 1) * TH],
                                       in_=psu[:, :TH], func=AF.Relu, bias=bia),
                                   waits=[te], inc=se)
                            self.pp.release(se)
                        self.wp.release(te)
                    for tc in range(c.NTCH):
                        psd, gd = self.pp.acquire(max(1, D // 512))
                        def dn_mm(E_, psd=psd, tc=tc, kind=kind, w2s=w2s):
                            last = None
                            for o, n in nnD:
                                for k in range(c.KH1):
                                    last = E_.matmul(
                                        psd[:, o:o + n],
                                        hact[:, k * TH + tc * 128:
                                             k * TH + (tc + 1) * 128],
                                        w2s[:, k * D + o: k * D + o + n],
                                        start=(k == 0),
                                        stop=(kind != "sh" and k == c.KH1 - 1))
                                if kind == "sh":
                                    E_.matmul(psd[:, o:o + n],
                                              self.onesb[0:1, :128],
                                              self.sb2r[0:1, o:o + n],
                                              start=False, stop=False)
                                    last = E_.matmul(
                                        psd[:, o:o + n],
                                        cw_fm[:, tc * 128:(tc + 1) * 128],
                                        self.eb2r[:, o:o + n],
                                        start=False, stop=True)
                            return last
                        b.step("tensor", dn_mm,
                               waits=[w2ch, se, w_ready, (self.dg, 80)] + gd,
                               inc=te)
                        if kind == "sh":
                            b.step("vector", lambda E_, tc=tc, psd=psd: E_.tensor_copy(
                                out=acc[:, tc * D:(tc + 1) * D], in_=psd[:, :D]),
                                waits=[te], inc=ve)
                        else:
                            b.step("vector", lambda E_, tc=tc, psd=psd, e=e:
                                   E_.scalar_tensor_tensor(
                                       out=acc[:, tc * D:(tc + 1) * D],
                                       in0=psd[:, :D],
                                       scalar=cw_tm[:, tc * c.E1 + e:
                                                    tc * c.E1 + e + 1],
                                       in1=acc[:, tc * D:(tc + 1) * D],
                                       op0=OP.mult, op1=OP.add), waits=[te], inc=ve)
                        self.pp.release(ve)
                    self.wp.release(te, rng=rw2)
                for tc in range(c.NTCH):
                    for g4 in range(0, c.KD, 4):
                        kk = min(4, c.KD - g4)
                        pst, gp = self.pp.acquire()
                        b.step("tensor", lambda E_, tc=tc, g4=g4, kk=kk, pst=pst:
                               [E_.transpose(
                                   pst[:, j * 128:(j + 1) * 128],
                                   acc[:, tc * D + (g4 + j) * 128:
                                       tc * D + (g4 + j + 1) * 128],
                                   self.idf[:, :]) for j in range(kk)][-1],
                               waits=[ve] + gp, inc=te)
                        rsl = self.R[:, :].rearrange("p (k t) -> p k t", k=c.KD)[
                            :, g4:g4 + kk, off + tc * 128: off + (tc + 1) * 128]
                        b.step("vector", lambda E_, pst=pst, rsl=rsl, kk=kk:
                               E_.scalar_tensor_tensor(
                                   out=rsl, in0=pst[:, :kk * 128].rearrange(
                                       "p (k t) -> p k t", k=kk),
                                   scalar=0.5, in1=rsl, op0=OP.mult, op1=OP.add),
                               waits=[te], inc=ve)
                        self.pp.release(ve)
            return (ve, ve.n)

        with ExitStack() as SP:
            preps = []
            rr = r_ready
            for half in range(2):
                xn_ready = self.ln(SP, half * TH, TH, rr)
                cw_tm, cw_fm, w_ready = self.topk_weights(
                    SP, xn_ready, half * TH, TH, 1)
                preps.append((xn_ready, cw_tm, cw_fm, w_ready))
                rr = (self.ve, self.ve.n)
            hact = b.sb("hact1", [128, c.KH1 * TH], BF16, stack=SB)
            acc = b.sb("acc1", [128, c.NTCH * D], F32, stack=SB)
            pd = (self.ve, self.ve.n)
            for half in range(2):
                r_ready = _half(half, r_ready, preps[half], pd)
            SB.close()
        return r_ready

    def l1attn(self, r_ready):
        b, c = self.b, self.c
        te, ve, se, dg = self.te, self.ve, self.se, self.dg
        T, D = c.T, c.D
        w = self.w
        nnT = nchunks(T)
        with ExitStack() as S:
            S2 = ExitStack()
            sq = Rot([b.sb(f"a1sq{i}", [128, T], BF16, stack=S2) for i in range(2)])
            xb = Rot([b.sb(f"a1xb{i}", [128, T], BF16, stack=S2) for i in range(2)])
            mb = b.sb("a1m", [128, T], F32, stack=S2)
            vr = b.sb("a1v", [128, T], F32, stack=S2)
            s1 = b.sb("a1s1", [128, T], F32, stack=S2)
            fb = b.sb("a1f", [128, T], F32, stack=S2)
            ps_sm, gsm = self.pp.acquire(max(1, T // 512))
            rs_sm = self.pp.last()
            ps_st, gst = self.pp.acquire(max(1, T // 512))
            rs_st = self.pp.last()
            for k in range(c.KD):
                a0 = k * T
                sq_, gq = sq.acquire()
                b.step("scalar", lambda E_, a0=a0, sq_=sq_: E_.activation(
                    out=sq_[:, :], in_=self.R[:, a0:a0 + T], func=AF.Square),
                    waits=[r_ready] + gq, inc=se)
                xb_, gx = xb.acquire()
                b.step("vector", lambda E_, a0=a0, xb_=xb_: E_.tensor_copy(
                    out=xb_[:, :], in_=self.R[:, a0:a0 + T]),
                    waits=[r_ready] + gx, inc=ve)
                b.step("tensor", lambda E_, xb_=xb_, k=k: [E_.matmul(
                    ps_sm[:, o:o + n], self.onesb[:, :], xb_[:, o:o + n],
                    start=(k == 0), stop=(k == c.KD - 1)) for o, n in nnT][-1],
                    waits=[ve] + (gsm if k == 0 else []), inc=te)
                xb.release(te)
                b.step("tensor", lambda E_, k=k, sq_=sq_: [E_.matmul(
                    ps_st[:, o:o + n], self.onesb[:, :], sq_[:, o:o + n],
                    start=(k == 0), stop=(k == c.KD - 1)) for o, n in nnT][-1],
                    waits=[se] + (gst if k == 0 else []), inc=te)
                sq.release(te)
            rD = 1.0 / c.D
            b.step("vector", lambda E_: E_.tensor_scalar(
                out=mb[:, :], in0=ps_sm[:, :T], scalar1=rD, scalar2=None,
                op0=OP.mult), waits=[te], inc=ve)
            self.pp.release(ve, rng=rs_sm)
            b.step("vector", lambda E_: E_.tensor_mul(vr[:, :], mb[:, :], mb[:, :]),
                   inc=ve)
            b.step("vector", lambda E_: E_.scalar_tensor_tensor(
                out=vr[:, :], in0=ps_st[:, :T], scalar=rD, in1=vr[:, :],
                op0=OP.mult, op1=OP.subtract), inc=ve)
            self.pp.release(ve, rng=rs_st)
            b.step("scalar", lambda E_: E_.activation(
                out=s1[:, :], in_=vr[:, :], func=AF.Ln, bias=1e-5),
                waits=[ve], inc=se)
            b.step("scalar", lambda E_: E_.activation(
                out=s1[:, :], in_=s1[:, :], func=AF.Exp, scale=-0.5), inc=se)
            b.step("vector", lambda E_: E_.tensor_scalar(
                out=fb[:, :], in0=vr[:, :], scalar1=float(c.D), scalar2=1e-30,
                op0=OP.mult, op1=OP.max), inc=ve)
            b.step("scalar", lambda E_: E_.activation(
                out=fb[:, :], in_=fb[:, :], func=AF.Ln), waits=[ve], inc=se)
            b.step("scalar", lambda E_: E_.activation(
                out=fb[:, :], in_=fb[:, :], func=AF.Exp, scale=0.5), inc=se)
            b.step("vector", lambda E_: E_.tensor_mul(fb[:, :], fb[:, :], s1[:, :]),
                   waits=[se], inc=ve)
            b.step("vector", lambda E_: E_.tensor_scalar(
                out=fb[:, :], in0=fb[:, :], scalar1=1e-12, scalar2=None,
                op0=OP.max), inc=ve)
            b.step("scalar", lambda E_: E_.activation(
                out=fb[:, :], in_=fb[:, :], func=AF.Ln), waits=[ve], inc=se)
            b.step("scalar", lambda E_: E_.activation(
                out=fb[:, :], in_=fb[:, :], func=AF.Exp, scale=-1.0,
                bias=self.lnD2), inc=se)
            b.step("vector", lambda E_: E_.tensor_mul(fb[:, :], fb[:, :], s1[:, :]),
                   waits=[se], inc=ve)
            for k in range(c.KD):
                a0 = k * T
                b.step("vector", lambda E_, a0=a0: E_.tensor_sub(
                    s1[:, :], self.R[:, a0:a0 + T], mb[:, :]), inc=ve)
                b.step("vector", lambda E_, k=k, a0=a0: E_.scalar_tensor_tensor(
                    out=self.XN[:, a0:a0 + T], in0=s1[:, :],
                    scalar=self.gmc[:, k:k + 1], in1=fb[:, :],
                    op0=OP.mult, op1=OP.mult), inc=ve)
            xn_ready = (ve, ve.n)
            S2.close()
            qb = b.sb("a1q", [128, T], F32, stack=S)
            kvb = b.sb("a1kv", [128, T], F32, stack=S)
            ab = b.sb("a1a", [128, T], F32, stack=S)
            snb = b.sb("a1sn", [128, T], F32, stack=S)

            for cc in range(c.KD):
                raws, gr, rch = self.wp.acquire()
                raws_rng = self.wp.last()
                for j in range(3):
                    b.dma("gpsimd", raws[:, j * D:(j + 1) * D],
                          w["W_qkva"][j * D + cc * 128: j * D + (cc + 1) * 128, :],
                          waits=gr if j == 0 else [], inc=rch)
                wtb, gw, _wc = self.wp.acquire()
                wtb_rng = self.wp.last()
                for j in range(3):
                    for k in range(c.KD):
                        pst, gp = self.pp.acquire()
                        b.step("tensor", lambda E_, j=j, k=k, pst=pst, raws=raws:
                               E_.transpose(
                                   pst[:, :64].bitcast(BF16),
                                   raws[:, j * D + k * 128: j * D + (k + 1) * 128],
                                   self.idb[:, :]),
                               waits=[rch] + gp,
                               inc=te)
                        b.step("vector", lambda E_, j=j, k=k, pst=pst, wtb=wtb:
                               E_.tensor_copy(
                                   out=wtb[:, (j * c.KD + k) * 128:
                                           (j * c.KD + k + 1) * 128],
                                   in_=pst[:, :64].bitcast(BF16)),
                               waits=[te] + (gw if j == 0 and k == 0 else []),
                               inc=ve)
                        self.pp.release(ve)
                self.wp.release(te, rng=raws_rng)
                for j, dstb in ((0, qb), (1, kvb), (2, ab)):
                    psq, gp = self.pp.acquire(max(1, T // 512))
                    b.step("tensor", lambda E_, j=j, psq=psq, wtb=wtb: [E_.matmul(
                        psq[:, o:o + n],
                        wtb[:, (j * c.KD + k) * 128:(j * c.KD + k + 1) * 128],
                        self.XN[:, k * T + o: k * T + o + n],
                        start=(k == 0), stop=(k == c.KD - 1))
                        for o, n in nnT for k in range(c.KD)][-1],
                        waits=[ve, xn_ready] + gp, inc=te)
                    if j == 2:
                        b.step("scalar", lambda E_, psq=psq: E_.activation(
                            out=ab[:, :], in_=psq[:, :T], func=AF.Sigmoid),
                            waits=[te], inc=se)
                        self.pp.release(se)
                    else:
                        b.step("vector", lambda E_, psq=psq, dstb=dstb: E_.tensor_copy(
                            out=dstb[:, :], in_=psq[:, :T]), waits=[te], inc=ve)
                        self.pp.release(ve)
                b.step("vector", lambda E_: E_.tensor_tensor_scan(
                    out=snb[:, :], data0=ab[:, :], data1=kvb[:, :], initial=0.0,
                    op0=OP.mult, op1=OP.add), waits=[se], inc=ve)
                b.step("vector", lambda E_: E_.tensor_mul(
                    snb[:, :], snb[:, :], qb[:, :]), inc=ve)
                b.step("vector", lambda E_, cc=cc: E_.tensor_add(
                    self.R[:, cc * T:(cc + 1) * T], self.R[:, cc * T:(cc + 1) * T],
                    snb[:, :]), inc=ve)
                self.wp.release(te, rng=wtb_rng)
        return (ve, ve.n)

    def load_x(self, si):
        b, c = self.b, self.c
        T, D = c.T, c.D
        xs = self.x_h[si * T:(si + 1) * T, :]
        for tc in range(c.NTC):
            buf, g, sch = self.stg.acquire()
            b.dma("sync", buf[:, :D], xs[tc * 128:(tc + 1) * 128, :],
                  waits=g, inc=sch)
            for g4 in range(0, c.KD, 4):
                kk = min(4, c.KD - g4)
                ps_, gp = self.pp.acquire()
                b.step("tensor", lambda E_, buf=buf, g4=g4, kk=kk, ps_=ps_:
                       [E_.transpose(ps_[:, j * 128:(j + 1) * 128],
                                     buf[:, (g4 + j) * 128:(g4 + j + 1) * 128],
                                     self.idf[:, :]) for j in range(kk)][-1],
                       waits=[sch, self.ge] + gp, inc=self.te)
                dst = self.R[:, :].rearrange("p (k t) -> p k t", k=c.KD)[
                    :, g4:g4 + kk, tc * 128:(tc + 1) * 128]
                b.step("vector", lambda E_, ps_=ps_, dst=dst, kk=kk: E_.tensor_copy(
                    out=dst, in_=ps_[:, :kk * 128].rearrange(
                        "p (k t) -> p k t", k=kk)), waits=[self.te], inc=self.ve)
                self.pp.release(self.ve)
            self.stg.release(self.te)
        return (self.ve, self.ve.n)

    def store_x(self, si, r_ready):
        b, c = self.b, self.c
        T, D = c.T, c.D
        xs = self.out_h[si * T:(si + 1) * T, :]
        for tc in range(c.NTC):
            buf, g, sch = self.stg.acquire()
            for g4 in range(0, c.KD, 4):
                kk = min(4, c.KD - g4)
                ps_, gp = self.pp.acquire()
                b.step("tensor", lambda E_, tc=tc, g4=g4, kk=kk, ps_=ps_:
                       [E_.transpose(
                           ps_[:, j * 128:(j + 1) * 128],
                           self.R[:, (g4 + j) * T + tc * 128:
                                  (g4 + j) * T + (tc + 1) * 128],
                           self.idf[:, :]) for j in range(kk)][-1],
                       waits=[r_ready, self.ge] + gp, inc=self.te)
                b.step("vector", lambda E_, ps_=ps_, buf=buf, g4=g4, kk=kk:
                       E_.tensor_copy(out=buf[:, g4 * 128:(g4 + kk) * 128],
                                      in_=ps_[:, :kk * 128]),
                       waits=[self.te] + (g if g4 == 0 else []), inc=self.ve)
                self.pp.release(self.ve)
            b.dma("sync", xs[tc * 128:(tc + 1) * 128, :], buf[:, :D],
                  waits=[self.ve], inc=sch)
            self.stg.release(sch)

    def build(self):
        self.declare_io()
        self.alloc()
        self.prologue()
        c = self.c
        for si in range(c.S):
            r_ready = self.load_x(si)
            for sub in c.sub:
                if sub == "f0":
                    r_ready = self.l0ffn(r_ready)
                elif sub == "a0":
                    r_ready = self.l0attn(r_ready)
                elif sub == "f1":
                    r_ready = self.l1ffn(r_ready)
                elif sub == "a1":
                    r_ready = self.l1attn(r_ready)
            self.store_x(si, r_ready)
        self.b.step("sync", lambda E: E.nop(),
                    waits=[(ch, ch.n) for ch in self.stg.chans])
        self.b.finalize()
        return self.nc


def kernel(**inputs):
    cfg = CFG()
    kb = KB(cfg)
    nc = kb.build()
    x = np.ascontiguousarray(inputs["L0_attn_x"], dtype=np.float32)
    B_, T_, D_ = x.shape
    per = B_ // cfg.n_cores
    in_maps = []
    for ci in range(cfg.n_cores):
        m = {"x": x[ci * per:(ci + 1) * per].reshape(per * T_, D_)}
        for k, v in inputs.items():
            if k == "L0_attn_x":
                continue
            a = np.ascontiguousarray(v, dtype=np.float32)
            if k == "L0_attn_diff_weight":
                a = a.reshape(1, 1)
            if k == "L0_attn_norm_w":
                a = a.reshape(-1)
            m[k] = a
        in_maps.append(m)
    res = run_bass_kernel_spmd(nc, in_maps, core_ids=list(range(cfg.n_cores)))
    out = np.concatenate(
        [res.results[ci]["out"].reshape(per, T_, D_) for ci in range(cfg.n_cores)],
        axis=0)
    return out


# version 06 (final)
# speedup vs baseline: 1.0763x; 1.0763x over previous
import os
os.environ.setdefault("GAUGE_NTFF_IGNORE_DMA", "1")

import numpy as np
from contextlib import ExitStack
import concourse.bass as bass
import concourse.mybir as mybir
from concourse.bass_utils import run_bass_kernel_spmd

F32 = mybir.dt.float32
BF16 = mybir.dt.bfloat16
AF = mybir.ActivationFunctionType
OP = mybir.AluOpType
AX = mybir.AxisListType

ENGINES = ("tensor", "vector", "scalar", "sync", "gpsimd")
EPS_RMS = 2.220446049250313e-16


class Ch:
    __slots__ = ("h", "n", "name", "owner")

    def __init__(self, h, name, owner=None):
        self.h = h
        self.n = 0
        self.name = name
        self.owner = owner


class B:
    def __init__(self, nc):
        self.nc = nc
        self.streams = {e: [] for e in ENGINES}
        self.stack = ExitStack()
        self.waited = {e: {} for e in ENGINES}
        self.names = set()
        self.engch = {}

    def _nm(self, name):
        i, n = 0, name
        while n in self.names:
            i += 1
            n = f"{name}_{i}"
        self.names.add(n)
        return n

    def ch(self, name, owner=None):
        h = self.stack.enter_context(self.nc.semaphore(self._nm(name)))
        return Ch(h, name, owner)

    def sb(self, name, shape, dtype, stack=None):
        return (stack or self.stack).enter_context(
            self.nc.sbuf_tensor(self._nm(name), shape, dtype))

    def step(self, eng, emit_fn, waits=(), inc=None, inc_n=1):
        wl = []
        ech = self.engch.get(eng)
        if ech is not None and ech.n > 0:
            wl.append((ech.h, ech.n))
        if ech is not None and inc is None:
            inc = ech
        for w in waits:
            if w is None:
                continue
            if isinstance(w, tuple):
                ch, v = w
            else:
                ch, v = w, w.n
            if v <= 0:
                continue
            seen = self.waited[eng].get(ch, -1)
            if v <= seen:
                continue
            self.waited[eng][ch] = v
            wl.append((ch.h, v))
        if inc is not None:
            inc_h = inc.h
            inc.n += inc_n
        else:
            inc_h = None

        def run(E, wl=wl, emit_fn=emit_fn, inc_h=inc_h, inc_n=inc_n):
            for h, v in wl:
                E.wait_ge(h, v)
            last = emit_fn(E)
            if inc_h is not None:
                last.then_inc(inc_h, inc_n)

        self.streams[eng].append(run)

    def dma(self, eng, out, in_, waits=(), inc=None, **kw):
        self.step(eng, lambda E: E.dma_start(out=out, in_=in_, **kw),
                  waits=waits, inc=inc, inc_n=16)

    def finalize(self):
        with self.nc.Block() as blk:
            for eng in ENGINES:
                fns = self.streams[eng]
                if not fns:
                    continue
                getattr(blk, eng)(lambda E, fns=fns: [f(E) for f in fns])
        self.stack.close()


class Rot:
    def __init__(self, bufs, chans=None):
        self.bufs = list(bufs)
        self.guard = [None] * len(self.bufs)
        self.chans = chans
        self.i = 0

    def acquire(self):
        s = self.i % len(self.bufs)
        self.i += 1
        g = self.guard[s]
        self.guard[s] = None
        self._last = s
        ret = self.bufs[s], ([g] if g is not None else [])
        if self.chans is not None:
            return ret + (self.chans[s],)
        return ret

    def release(self, ch):
        self.guard[self._last] = (ch, ch.n)


class Pool:

    def __init__(self, t, nslots, cols, chans=None):
        self.t = t
        self.cols = cols
        self.guard = [None] * nslots
        self.held = [False] * nslots
        self.chans = chans
        self.n = nslots
        self.i = 0

    def acquire(self, k=1):
        cands = [((self.i + off) % self.n) // k * k for off in range(self.n)]
        cands += [(self.i + off) % self.n for off in range(self.n)]
        for s in cands:
            if s + k <= self.n and not any(self.held[s:s + k]):
                break
        else:
            raise RuntimeError("pool exhausted")
        self.i = (s + k) % self.n
        g = [x for x in self.guard[s:s + k] if x is not None]
        for j in range(s, s + k):
            self.guard[j] = None
            self.held[j] = True
        self._last = (s, k)
        ret = self.t[:, s * self.cols:(s + k) * self.cols], g
        if self.chans is not None:
            return ret + (self.chans[s],)
        return ret

    def release(self, ch, rng=None):
        s, k = self._last if rng is None else rng
        for j in range(s, s + k):
            self.guard[j] = (ch, ch.n)
            self.held[j] = False

    def last(self):
        return self._last


def nchunks(total, mx=512):
    out, off = [], 0
    while off < total:
        c = min(mx, total - off)
        out.append((off, c))
        off += c
    return out


class CFG:
    def __init__(self, D=1024, T=2048, S=4, E0=8, I0=4096, E1=15, H1=1024,
                 n_cores=8, sub=None):
        self.D, self.T, self.S = D, T, S
        self.E0, self.I0, self.E1, self.H1 = E0, I0, E1, H1
        self.KD = D // 128
        self.KI0 = I0 // 128
        self.KH1 = H1 // 128
        self.H = D // 128
        self.TH = T // 2
        self.NTC = T // 128
        self.NTCH = self.TH // 128
        self.n_cores = n_cores
        self.MB = 4
        self.QG = 1
        self.sub = sub if sub is not None else ["f0", "a0", "f0", "f1", "a1", "f1"]


class KB:
    def __init__(self, cfg: CFG):
        self.c = cfg
        self.nc = bass.Bass("TRN2", target_bir_lowering=False)
        self.b = B(self.nc)

    def declare_io(self):
        c, nc = self.c, self.nc
        D = c.D
        self.x_h = nc.dram_tensor("x", [c.S * c.T, D], F32, kind="ExternalInput").ap()
        self.out_h = nc.dram_tensor("out", [c.S * c.T, D], F32, kind="ExternalOutput").ap()
        dram = lambda n, sh: nc.dram_tensor(n, sh, F32, kind="ExternalInput").ap()
        self.w = {
            "gate_w": dram("L0_ffn_gate_w", [D, c.E0]),
            "gate_b": dram("L0_ffn_bias", [c.E0]),
            "up_w": dram("L0_ffn_up_w", [c.E0, D, c.I0]),
            "up_b": dram("L0_ffn_up_b", [c.E0, c.I0]),
            "down_w": dram("L0_ffn_down_w", [c.E0, c.I0, D]),
            "down_b": dram("L0_ffn_down_b", [c.E0, D]),
            "sw_up": dram("L0_ffn_sw_up", [D, c.I0]),
            "sb_up": dram("L0_ffn_sb_up", [c.I0]),
            "sw_down": dram("L0_ffn_sw_down", [c.I0, D]),
            "sb_down": dram("L0_ffn_sb_down", [D]),
            "Wq": dram("L0_attn_Wq", [D, D]),
            "Wk": dram("L0_attn_Wk", [D, D]),
            "Wv": dram("L0_attn_Wv", [D, D]),
            "Wo": dram("L0_attn_Wo", [D, D]),
            "bo": dram("L0_attn_bo", [D]),
            "norm_w": dram("L0_attn_norm_w", [D]),
            "dw": dram("L0_attn_diff_weight", [1, 1]),
            "W_qkva": dram("L1_attn_W_qkva", [3 * D, D]),
            "gamma": dram("L1_attn_gamma", [D]),
            "router_W": dram("L1_ffn_router_W", [D, c.E1]),
            "router_b": dram("L1_ffn_router_b", [c.E1]),
            "sW1": dram("L1_ffn_sW1", [D, c.H1]),
            "sb1": dram("L1_ffn_sb1", [c.H1]),
            "sW2": dram("L1_ffn_sW2", [c.H1, D]),
            "sb2": dram("L1_ffn_sb2", [D]),
            "eW1": dram("L1_ffn_eW1", [c.E1, D, c.H1]),
            "eb1": dram("L1_ffn_eb1", [c.E1, c.H1]),
            "eW2": dram("L1_ffn_eW2", [c.E1, c.H1, D]),
            "eb2": dram("L1_ffn_eb2", [c.E1, D]),
        }

    def alloc(self):
        b, c = self.b, self.c
        self.te = b.ch("te", owner="tensor")
        self.ve = b.ch("ve", owner="vector")
        self.se = b.ch("se", owner="scalar")
        self.ge = b.ch("ge", owner="gpsimd")
        self.dg = b.ch("dg")
        self.ds = b.ch("ds")
        b.engch = {"tensor": self.te, "vector": self.ve, "scalar": self.se,
                   "gpsimd": self.ge}

        D, T = c.D, c.T
        self.R = b.sb("R", [128, c.KD * T], F32)
        self.XN = b.sb("XN", [128, c.KD * T], BF16)
        self.stg = Rot([b.sb(f"stg{i}", [128, D], F32) for i in range(1)],
                       chans=[b.ch(f"sch{i}") for i in range(1)])
        self.wp = Pool(b.sb("wpool", [128, 5 * 4096], BF16), 5, 4096,
                       chans=[b.ch(f"wch{i}") for i in range(5)])
        self.pp = Pool(self.b.stack.enter_context(
            self.nc.psum_tensor("pspool", [128, 8 * 512], F32)), 8, 512)

        self.onesb = b.sb("onesb", [128, 128], BF16)
        self.onesf = b.sb("onesf", [128, 128], F32)
        self.idb = b.sb("idb", [128, 128], BF16)
        self.idf = b.sb("idf", [128, 128], F32)
        self.gateW = b.sb("gateW", [128, c.KD * c.E0], BF16)
        self.routW = b.sb("routW", [128, c.KD * c.E1], BF16)
        self.gateB = b.sb("gateB", [max(c.E0, c.E1), 2], F32)
        self.upB = b.sb("upB", [128, c.E0 * c.KI0], F32)
        self.swUpB = b.sb("swUpB", [128, c.KI0], F32)
        self.sbDs = b.sb("sbDs", [128, c.KD], F32)
        self.dnBr = b.sb("dnBr", [c.E0, D], BF16)
        self.eb1c = b.sb("eb1c", [128, c.E1 * c.KH1], F32)
        self.sb1c = b.sb("sb1c", [128, c.KH1], F32)
        self.sb2r = b.sb("sb2r", [1, D], BF16)
        self.eb2r = b.sb("eb2r", [c.E1, D], BF16)
        self.boS = b.sb("boS", [128, c.KD], F32)
        self.nwc = b.sb("nwc", [128, c.KD], F32)
        self.gmc = b.sb("gmc", [128, c.KD], F32)
        self.sc1 = b.sb("sc1", [128, 1], F32)
        self.dwn = b.sb("dwn", [128, 1], F32)
        self.cv = b.sb("cv", [128, 3], F32)
        self.lnD2 = 0.5 * float(np.log(c.D))
        for i, v in enumerate((1e-5, EPS_RMS, self.lnD2)):
            self.nc.const_aps.register(F32, v, self.cv[:, i:i + 1])

    def prologue(self):
        b, c = self.b, self.c
        te, ve, se, ge, ds = self.te, self.ve, self.se, self.ge, self.ds
        for i, v in enumerate((1e-5, EPS_RMS, self.lnD2)):
            b.step("vector", lambda E, i=i, v=v: E.memset(self.cv[:, i:i + 1], v),
                   inc=ve)
        b.step("vector", lambda E: E.memset(self.onesb[:, :], 1.0), inc=ve)
        b.step("vector", lambda E: E.memset(self.onesf[:, :], 1.0), inc=ve)
        b.step("vector", lambda E: E.memset(self.idb[:, :], 0.0), inc=ve)
        b.step("vector", lambda E: E.memset(self.idf[:, :], 0.0), inc=ve)
        for t in (self.idb, self.idf):
            b.step("gpsimd", lambda E, t=t: E.affine_select(
                out=t[:, :], in_=t[:, :], compare_op=OP.not_equal, fill=1.0,
                base=0, pattern=[[-1, 128]], channel_multiplier=1),
                waits=[ve], inc=ge)

        with ExitStack() as S:
            tmp = b.sb("pro_tmp", [32, 128], F32, stack=S)

            def feat_vec(src_ap, dst, dcol, n):
                kk = n // 128
                ps_, gp = self.pp.acquire()
                b.dma("sync", tmp[:kk, :], src_ap.rearrange("(k p) -> k p", p=128),
                      waits=[(te, te.n), (ds, ds.n)], inc=ds)
                b.step("tensor", lambda E, kk=kk, ps_=ps_: E.transpose(
                    ps_[:, :kk], tmp[:kk, :], self.idf[:kk, :kk]),
                    waits=[ds, ge] + gp, inc=te)
                b.step("vector", lambda E, kk=kk, ps_=ps_, dst=dst, dcol=dcol:
                       E.tensor_copy(out=dst[:, dcol:dcol + kk], in_=ps_[:, :kk]),
                       waits=[te], inc=ve)
                self.pp.release(ve)

            w = self.w
            feat_vec(w["sb_up"], self.swUpB, 0, c.I0)
            for e in range(c.E0):
                feat_vec(w["up_b"][e, :], self.upB, e * c.KI0, c.I0)
            feat_vec(w["sb_down"], self.sbDs, 0, c.D)
            for e in range(c.E1):
                feat_vec(w["eb1"][e, :], self.eb1c, e * c.KH1, c.H1)
            feat_vec(w["sb1"], self.sb1c, 0, c.H1)
            feat_vec(w["bo"], self.boS, 0, c.D)
            feat_vec(w["norm_w"], self.nwc, 0, c.D)
            feat_vec(w["gamma"], self.gmc, 0, c.D)

            b.dma("gpsimd", self.sb2r[0:1, :], w["sb2"][None, :], inc=self.dg)
            b.dma("gpsimd", self.eb2r[:, :], w["eb2"], inc=self.dg)
            b.dma("gpsimd", self.dnBr[:, :], w["down_b"], inc=self.dg)
            b.dma("sync", self.gateB[:c.E0, 0:1], w["gate_b"][:, None], inc=ds,
                  waits=[(ds, ds.n)], allow_slow_non_contiguous=True)
            b.dma("sync", self.gateB[:c.E1, 1:2], w["router_b"][:, None], inc=ds,
                  waits=[(ds, ds.n)], allow_slow_non_contiguous=True)
            b.dma("gpsimd", self.gateW[:, :].rearrange("p (k e) -> p k e", k=c.KD),
                  w["gate_w"].rearrange("(k p) e -> p k e", p=128), inc=self.dg)
            b.dma("gpsimd", self.routW[:, :].rearrange("p (k e) -> p k e", k=c.KD),
                  w["router_W"].rearrange("(k p) e -> p k e", p=128), inc=self.dg)

            dwt = b.sb("pro_dw", [1, 128], F32, stack=S)
            ps_, gp = self.pp.acquire()
            b.dma("sync", dwt[0:1, 0:1], w["dw"], waits=[(ds, ds.n)], inc=ds)
            b.step("vector", lambda E: E.tensor_scalar(
                out=dwt[0:1, :], in0=self.onesf[0:1, :], scalar1=dwt[0:1, 0:1],
                scalar2=None, op0=OP.mult), waits=[ds], inc=ve)
            b.step("tensor", lambda E, ps_=ps_: E.transpose(
                ps_[:, 0:1], dwt[0:1, :], self.idf[0:1, 0:1]),
                waits=[ve, ge] + gp, inc=te)
            b.step("vector", lambda E, ps_=ps_: E.tensor_scalar(
                out=self.sc1[:, :], in0=ps_[:, 0:1], scalar1=-1.0, scalar2=1.0,
                op0=OP.mult, op1=OP.add), waits=[te], inc=ve)
            b.step("vector", lambda E, ps_=ps_: E.tensor_scalar(
                out=self.dwn[:, :], in0=ps_[:, 0:1], scalar1=-1.0, scalar2=None,
                op0=OP.mult), inc=ve)
            self.pp.release(ve)
            b.step("vector", lambda E: E.tensor_scalar(
                out=self.boS[:, :], in0=self.boS[:, :], scalar1=self.sc1[:, 0:1],
                scalar2=None, op0=OP.mult), inc=ve)
            b.step("vector", lambda E: E.tensor_scalar(
                out=self.sbDs[:, :], in0=self.sbDs[:, :], scalar1=0.1, scalar2=None,
                op0=OP.mult), inc=ve)

    def ln(self, S, off, L, r_ready):
        b, c = self.b, self.c
        te, ve, se = self.te, self.ve, self.se
        T = c.T
        nn = nchunks(L)
        S = self._ln_stack = ExitStack()
        sq = Rot([b.sb(f"ln_sq{i}", [128, L], BF16, stack=S) for i in range(2)])
        xb = Rot([b.sb(f"ln_xb{i}", [128, L], BF16, stack=S) for i in range(2)])
        mb = b.sb("ln_m", [128, L], F32, stack=S)
        ab = b.sb("ln_a", [128, L], F32, stack=S)
        tb = b.sb("ln_t", [128, L], F32, stack=S)
        nsl = max(1, L // 512)
        pss, g1 = self.pp.acquire(nsl)
        rs1 = self.pp.last()
        psq, g2 = self.pp.acquire(nsl)
        rs2 = self.pp.last()
        for k in range(c.KD):
            a0 = k * T + off
            sqb, gq = sq.acquire()
            b.step("scalar", lambda E, a0=a0, sqb=sqb: E.activation(
                out=sqb[:, :], in_=self.R[:, a0:a0 + L], func=AF.Square),
                waits=[r_ready] + gq, inc=se)
            xbb, gx = xb.acquire()
            b.step("vector", lambda E, a0=a0, xbb=xbb: E.tensor_copy(
                out=xbb[:, :], in_=self.R[:, a0:a0 + L]), waits=[r_ready] + gx,
                inc=ve)
            b.step("tensor", lambda E, k=k, xbb=xbb: [E.matmul(
                pss[:, o:o + n], self.onesb[:, :], xbb[:, o:o + n],
                start=(k == 0), stop=(k == c.KD - 1)) for o, n in nn][-1],
                waits=[ve] + (g1 if k == 0 else []), inc=te)
            xb.release(te)
            b.step("tensor", lambda E, k=k, sqb=sqb: [E.matmul(
                psq[:, o:o + n], self.onesb[:, :], sqb[:, o:o + n],
                start=(k == 0), stop=(k == c.KD - 1)) for o, n in nn][-1],
                waits=[se] + (g2 if k == 0 else []), inc=te)
            sq.release(te)
        rD = 1.0 / c.D
        b.step("vector", lambda E: E.tensor_scalar(
            out=mb[:, :], in0=pss[:, :L], scalar1=rD, scalar2=None, op0=OP.mult),
            waits=[te], inc=ve)
        b.step("vector", lambda E: E.tensor_mul(tb[:, :], mb[:, :], mb[:, :]), inc=ve)
        b.step("vector", lambda E: E.scalar_tensor_tensor(
            out=tb[:, :], in0=psq[:, :L], scalar=rD, in1=tb[:, :],
            op0=OP.mult, op1=OP.subtract), inc=ve)
        self.pp.release(ve, rng=rs1)
        self.pp.release(ve, rng=rs2)
        b.step("scalar", lambda E: E.activation(
            out=ab[:, :], in_=tb[:, :], func=AF.Ln, bias=1e-5), waits=[ve], inc=se)
        b.step("scalar", lambda E: E.activation(
            out=ab[:, :], in_=ab[:, :], func=AF.Exp, scale=-0.5), inc=se)
        for k in range(c.KD):
            a0 = k * T + off
            b.step("vector", lambda E, a0=a0: E.tensor_sub(
                tb[:, :], self.R[:, a0:a0 + L], mb[:, :]), waits=[se], inc=ve)
            b.step("vector", lambda E, a0=a0: E.tensor_mul(
                self.XN[:, a0:a0 + L], tb[:, :], ab[:, :]), inc=ve)
        S.close()
        return (ve, ve.n)

    def topk_weights(self, S, xn_ready, off, L, which):
        b, c = self.b, self.c
        te, ve, se = self.te, self.ve, self.se
        E = c.E0 if which == 0 else c.E1
        Wt = self.gateW if which == 0 else self.routW
        NT = L // 128
        nn = nchunks(L)
        w_tm = b.sb("wtm", [128, NT * E], F32, stack=S)
        w_fm = b.sb("wfm", [E, L], BF16, stack=S)
        S2 = ExitStack()
        sco = b.sb("sco", [E, L], F32, stack=S2)
        prb = b.sb("prb", [128, E], F32, stack=S2)
        m1 = b.sb("gm", [128, 6], F32, stack=S2)
        msk = b.sb("gmsk", [128, E], F32, stack=S2)
        wv = b.sb("gwv", [128, E], F32, stack=S2)

        nsl = max(1, L // 512)
        psg, gg = self.pp.acquire(nsl)
        T = c.T
        b.step("tensor", lambda E_: [E_.matmul(
            psg[:E, o:o + n], Wt[:, k * E:(k + 1) * E],
            self.XN[:, k * T + off + o: k * T + off + o + n],
            start=(k == 0), stop=(k == c.KD - 1))
            for o, n in nn for k in range(c.KD)][-1],
            waits=[xn_ready, (self.dg, 80)] + gg, inc=te)
        bias = self.gateB[:E, 0:1] if which == 0 else self.gateB[:E, 1:2]
        if which == 0:
            b.step("scalar", lambda E_: E_.activation(
                out=sco[:, :], in_=psg[:E, :L], func=AF.Sigmoid, bias=bias),
                waits=[te], inc=se)
        else:
            b.step("scalar", lambda E_: E_.activation(
                out=sco[:, :], in_=psg[:E, :L], func=AF.Identity, bias=bias),
                waits=[te], inc=se)
        self.pp.release(se)

        for tc in range(NT):
            pst, gt = self.pp.acquire()
            b.step("tensor", lambda E_, tc=tc, pst=pst: E_.transpose(
                pst[:, :E], sco[:, tc * 128:(tc + 1) * 128], self.idf[:E, :E]),
                waits=[se] + gt, inc=te)
            s_ap = pst[:, :E]
            if which == 1:
                b.step("vector", lambda E_, s_ap=s_ap: E_.tensor_reduce(
                    out=m1[:, 0:1], in_=s_ap, axis=AX.X, op=OP.max, negate=True),
                    waits=[te], inc=ve)
                b.step("scalar", lambda E_, s_ap=s_ap: E_.activation(
                    out=prb[:, :E], in_=s_ap, func=AF.Exp, bias=m1[:, 0:1],
                    accum_out=m1[:, 1:2]), waits=[ve], inc=se)
                b.step("vector", lambda E_: E_.reciprocal(
                    out=m1[:, 2:3], in_=m1[:, 1:2]), waits=[se], inc=ve)
                b.step("vector", lambda E_: E_.tensor_scalar(
                    out=prb[:, :E], in0=prb[:, :E], scalar1=m1[:, 2:3],
                    scalar2=None, op0=OP.mult), inc=ve)
                src = prb[:, :E]
                kk = 3
            else:
                b.step("vector", lambda E_, s_ap=s_ap: E_.tensor_copy(
                    out=prb[:, :E], in_=s_ap), waits=[te], inc=ve)
                src = prb[:, :E]
                kk = 2
            self.pp.release(ve)
            b.step("vector", lambda E_: E_.tensor_copy(out=wv[:, :E], in_=src),
                   inc=ve)
            for j in range(kk):
                b.step("vector", lambda E_, j=j: E_.tensor_reduce(
                    out=m1[:, 3 + j:4 + j], in_=wv[:, :E], axis=AX.X, op=OP.max),
                    inc=ve)
                if j < kk - 1:
                    b.step("vector", lambda E_, j=j: E_.tensor_scalar(
                        out=msk[:, :E], in0=wv[:, :E], scalar1=m1[:, 3 + j:4 + j],
                        scalar2=None, op0=OP.is_lt), inc=ve)
                    b.step("vector", lambda E_: E_.tensor_mul(
                        wv[:, :E], msk[:, :E], wv[:, :E]), inc=ve)
            b.step("vector", lambda E_, kk=kk: E_.tensor_scalar(
                out=msk[:, :E], in0=src, scalar1=m1[:, 2 + kk:3 + kk],
                scalar2=None, op0=OP.is_ge), inc=ve)
            dst = w_tm[:, tc * E:(tc + 1) * E]
            b.step("vector", lambda E_, dst=dst: E_.tensor_mul(
                dst, msk[:, :E], src), inc=ve)
            if which == 0:
                b.step("vector", lambda E_: E_.tensor_add(
                    m1[:, 5:6], m1[:, 3:4], m1[:, 4:5]), inc=ve)
                b.step("vector", lambda E_: E_.tensor_scalar(
                    out=m1[:, 5:6], in0=m1[:, 5:6], scalar1=1e-6, scalar2=None,
                    op0=OP.add), inc=ve)
                b.step("vector", lambda E_: E_.reciprocal(
                    out=m1[:, 5:6], in_=m1[:, 5:6]), inc=ve)
                b.step("vector", lambda E_, dst=dst: E_.tensor_scalar(
                    out=dst, in0=dst, scalar1=m1[:, 5:6], scalar2=None,
                    op0=OP.mult), inc=ve)
            psb, gb = self.pp.acquire()
            b.step("tensor", lambda E_, dst=dst, psb=psb: E_.transpose(
                psb[:E, :128], dst, self.idf[:, :]), waits=[ve] + gb, inc=te)
            b.step("vector", lambda E_, tc=tc, psb=psb: E_.tensor_copy(
                out=w_fm[:, tc * 128:(tc + 1) * 128], in_=psb[:E, :128]),
                waits=[te], inc=ve)
            self.pp.release(ve)
        S2.close()
        return w_tm, w_fm, (ve, ve.n)

    def l0ffn(self, r_ready):
        b, c = self.b, self.c
        te, ve, se, dg = self.te, self.ve, self.se, self.dg
        T, TH, D = c.T, c.TH, c.D
        w = self.w

        KIH = c.KI0 // 4
        SB = ExitStack()
        hact = acc = be = None

        def _half(half, r_ready, prep, preps_done):
            off = half * TH
            nn = nchunks(TH)
            first_mm = [preps_done]
            with ExitStack() as S:
                xn_ready, w_tm, w_fm, w_ready = prep
                tmp = Rot([self.stg.bufs[0][:, :TH]])
                units = [("sh", w["sw_up"], w["sw_down"])] + [
                    ("ex", w["up_w"][e], w["down_w"][e]) for e in range(c.E0)]
                for ui, (kind, wu, wd) in enumerate(units):
                    e = ui - 1
                    if kind == "ex":
                        for tc in range(c.NTCH):
                            psb, gb = self.pp.acquire()
                            b.step("vector", lambda E_, tc=tc, e=e: E_.tensor_scalar(
                                out=tmp.bufs[0][:, :128], in0=self.onesf[:, :],
                                scalar1=w_tm[:, tc * c.E0 + e:tc * c.E0 + e + 1],
                                scalar2=0.1, op0=OP.mult, op1=OP.mult),
                                waits=[w_ready, (te, te.n)], inc=ve)
                            b.step("tensor", lambda E_, psb=psb: E_.transpose(
                                psb[:, :128], tmp.bufs[0][:, :128], self.idf[:, :]),
                                waits=[ve] + gb, inc=te)
                            b.step("vector", lambda E_, tc=tc, psb=psb: E_.tensor_copy(
                                out=be[:, tc * 128:(tc + 1) * 128], in_=psb[:, :128]),
                                waits=[te], inc=ve)
                            self.pp.release(ve)
                    for ih in range(c.KI0 // KIH):
                        for mb0 in range(0, KIH, c.MB):
                            mbn = min(c.MB, KIH - mb0)
                            gm0 = ih * KIH + mb0
                            slot, gw, wch = self.wp.acquire()
                            b.dma("gpsimd",
                                  slot[:, :c.KD * mbn * 128].rearrange(
                                      "p (k m) -> p k m", k=c.KD),
                                  wu[:, gm0 * 128:(gm0 + mbn) * 128].rearrange(
                                      "(k p) m -> p k m", p=128),
                                  waits=gw, inc=wch)
                            for mm in range(mbn):
                                m = mb0 + mm
                                gm = gm0 + mm
                                psu, gu = self.pp.acquire(max(1, TH // 512))
                                b.step("tensor", lambda E_, slot=slot, mm=mm,
                                       mbn=mbn, psu=psu: [E_.matmul(
                                           psu[:, o:o + n],
                                           slot[:, (k * mbn + mm) * 128:
                                                (k * mbn + mm) * 128 + 128],
                                           self.XN[:, k * T + off + o:
                                                   k * T + off + o + n],
                                           start=(k == 0), stop=(k == c.KD - 1))
                                           for o, n in nn for k in range(c.KD)][-1],
                                       waits=[wch, xn_ready] + gu + first_mm,
                                       inc=te)
                                bia = (self.swUpB[:, gm:gm + 1] if kind == "sh"
                                       else self.upB[:, e * c.KI0 + gm:
                                                     e * c.KI0 + gm + 1])
                                b.step("scalar", lambda E_, m=m, psu=psu, bia=bia:
                                       E_.activation(
                                           out=hact[:, m * TH:(m + 1) * TH],
                                           in_=psu[:, :TH], func=AF.Gelu, bias=bia),
                                       waits=[te], inc=se)
                                self.pp.release(se)
                            self.wp.release(te)
                        for d in range(c.KD):
                            slot, gw, wch = self.wp.acquire()
                            for rc in range(0, KIH, 8):
                                rn = min(8, KIH - rc)
                                gr0 = (ih * KIH + rc) * 128
                                b.dma("gpsimd",
                                      slot[:, rc * 128:(rc + rn) * 128].rearrange(
                                          "p (k m) -> p k m", k=rn),
                                      wd[gr0:gr0 + rn * 128,
                                         d * 128:(d + 1) * 128].rearrange(
                                          "(k p) m -> p k m", p=128),
                                      waits=gw if rc == 0 else [], inc=wch)
                            psd, gd = self.pp.acquire(max(1, TH // 512))
                            last_half = ih == 1
                            def dn_mm(E_, slot=slot, psd=psd, kind=kind, d=d,
                                      last_half=last_half):
                                last = None
                                for o, n in nn:
                                    for k in range(KIH):
                                        last = E_.matmul(
                                            psd[:, o:o + n],
                                            slot[:, k * 128:(k + 1) * 128],
                                            hact[:, k * TH + o: k * TH + o + n],
                                            start=(k == 0),
                                            stop=(k == KIH - 1 and
                                                  not (kind == "sh" and last_half)))
                                    if kind == "sh" and last_half:
                                        last = E_.matmul(
                                            psd[:, o:o + n],
                                            self.dnBr[:c.E0, d * 128:(d + 1) * 128],
                                            w_fm[:, o:o + n], start=False, stop=True)
                                return last
                            b.step("tensor", dn_mm,
                                   waits=[wch, se, w_ready, (self.dg, 80)] + gd,
                                   inc=te)
                            self.wp.release(te)
                            if kind == "sh" and ih == 0:
                                b.step("scalar", lambda E_, d=d, psd=psd:
                                       E_.activation(
                                           out=acc[:, d * TH:(d + 1) * TH],
                                           in_=psd[:, :TH], func=AF.Identity,
                                           scale=0.1, bias=self.sbDs[:, d:d + 1]),
                                       waits=[te, (ve, ve.n)], inc=se)
                                self.pp.release(se)
                            elif kind == "sh":
                                b.step("vector", lambda E_, d=d, psd=psd:
                                       E_.scalar_tensor_tensor(
                                           out=acc[:, d * TH:(d + 1) * TH],
                                           in0=psd[:, :TH], scalar=0.1,
                                           in1=acc[:, d * TH:(d + 1) * TH],
                                           op0=OP.mult, op1=OP.add),
                                       waits=[te, se], inc=ve)
                                self.pp.release(ve)
                            else:
                                tb, gt = tmp.acquire()
                                b.step("vector", lambda E_, psd=psd, tb=tb:
                                       E_.tensor_mul(tb[:, :], psd[:, :TH],
                                                     be[:, :]),
                                       waits=[te, se] + gt, inc=ve)
                                b.step("vector", lambda E_, d=d, tb=tb:
                                       E_.tensor_add(
                                           acc[:, d * TH:(d + 1) * TH],
                                           acc[:, d * TH:(d + 1) * TH], tb[:, :]),
                                       inc=ve)
                                tmp.release(ve)
                                self.pp.release(ve)
                    dn_read = (te, te.n)
                mxs = b.sb("l0mxs", [128, c.NTCH], F32, stack=S)
                bn = be
                for tc in range(c.NTCH):
                    pst, gt = self.pp.acquire(max(1, c.KD // 4))
                    b.step("tensor", lambda E_, tc=tc, pst=pst: [E_.transpose(
                        pst[:, k * 128:(k + 1) * 128],
                        acc[:, k * TH + tc * 128: k * TH + (tc + 1) * 128],
                        self.idf[:, :]) for k in range(c.KD)][-1],
                        waits=[ve, se] + gt, inc=te)
                    b.step("vector", lambda E_, tc=tc, pst=pst: E_.tensor_reduce(
                        out=mxs[:, tc:tc + 1], in_=pst[:, :c.KD * 128], axis=AX.X,
                        op=OP.max, apply_absolute_value=True),
                        waits=[te], inc=ve)
                    self.pp.release(ve)
                b.step("vector", lambda E_: E_.tensor_scalar(
                    out=mxs[:, :], in0=mxs[:, :], scalar1=1e-6, scalar2=None,
                    op0=OP.add), inc=ve)
                b.step("vector", lambda E_: E_.reciprocal(
                    out=mxs[:, :], in_=mxs[:, :]), inc=ve)
                for tc in range(c.NTCH):
                    psb, gb = self.pp.acquire()
                    b.step("vector", lambda E_, tc=tc: E_.tensor_scalar(
                        out=tmp.bufs[0][:, :128], in0=self.onesf[:, :],
                        scalar1=mxs[:, tc:tc + 1], scalar2=None, op0=OP.mult),
                        waits=[(te, te.n)], inc=ve)
                    b.step("tensor", lambda E_, psb=psb: E_.transpose(
                        psb[:, :128], tmp.bufs[0][:, :128], self.idf[:, :]),
                        waits=[ve] + gb, inc=te)
                    b.step("vector", lambda E_, tc=tc, psb=psb: E_.tensor_copy(
                        out=bn[:, tc * 128:(tc + 1) * 128], in_=psb[:, :128]),
                        waits=[te], inc=ve)
                    self.pp.release(ve)
                for k in range(c.KD):
                    tb, gt = tmp.acquire()
                    b.step("vector", lambda E_, k=k, tb=tb: E_.tensor_mul(
                        tb[:, :], acc[:, k * TH:(k + 1) * TH], bn[:, :]),
                        waits=gt, inc=ve)
                    b.step("vector", lambda E_, k=k, tb=tb: E_.scalar_tensor_tensor(
                        out=self.R[:, k * T + off: k * T + off + TH], in0=tb[:, :],
                        scalar=0.5, in1=self.R[:, k * T + off: k * T + off + TH],
                        op0=OP.mult, op1=OP.add), inc=ve)
                    tmp.release(ve)
            return (ve, ve.n)

        with ExitStack() as SP:
            preps = []
            rr = r_ready
            for half in range(2):
                xn_ready = self.ln(SP, half * TH, TH, rr)
                w_tm, w_fm, w_ready = self.topk_weights(
                    SP, xn_ready, half * TH, TH, 0)
                preps.append((xn_ready, w_tm, w_fm, w_ready))
                rr = (self.ve, self.ve.n)
            hact = b.sb("hact", [128, KIH * TH], BF16, stack=SB)
            acc = b.sb("acc", [128, c.KD * TH], F32, stack=SB)
            be = b.sb("be", [128, TH], F32, stack=SB)
            pd = (self.ve, self.ve.n)
            for half in range(2):
                r_ready = _half(half, r_ready, preps[half], pd)
            SB.close()
        return r_ready

    def l0attn(self, r_ready):
        b, c = self.b, self.c
        te, ve, se, dg = self.te, self.ve, self.se, self.dg
        T, D, F = c.T, c.D, 64
        w = self.w
        scale = float(F) ** -0.5
        with ExitStack() as S:
            xn_ready = self.ln(S, 0, T, r_ready)
            o_fm = b.sb("o_fm", [128, c.KD * T], BF16, stack=S)
            SH = ExitStack()
            qh = b.sb("qh", [128, T], BF16, stack=SH)
            kh = b.sb("kh", [128, T], BF16, stack=SH)
            vh = b.sb("vh", [128, T], BF16, stack=SH)
            p0 = b.sb("p0", [128, T], BF16, stack=SH)
            att = b.sb("att", [128, T], BF16, stack=SH)
            attT = b.sb("attT", [128, c.NTC * c.QG * 128], BF16, stack=SH)
            sc = b.sb("attsc", [128, 12], F32, stack=SH)
            ps_s, gs = self.pp.acquire(max(1, T // 512))
            rs_s = self.pp.last()
            exp_done = (se, se.n)
            nnT = nchunks(T)
            for h in range(c.H):
                raws, gr, rch = self.wp.acquire()
                raws_rng = self.wp.last()
                for j, wm in enumerate((w["Wq"], w["Wk"], w["Wv"])):
                    b.dma("gpsimd", raws[:, j * D:(j + 1) * D],
                          wm[h * 128:(h + 1) * 128, :], waits=gr if j == 0 else [],
                          inc=rch)
                wtb, gw, _wc = self.wp.acquire()
                wtb_rng = self.wp.last()
                for j in range(3):
                    for k in range(c.KD):
                        pst, gp = self.pp.acquire()
                        b.step("tensor", lambda E_, j=j, k=k, pst=pst, raws=raws:
                               E_.transpose(pst[:, :64].bitcast(BF16),
                                            raws[:, j * D + k * 128: j * D + (k + 1) * 128],
                                            self.idb[:, :]),
                               waits=[rch] + gp,
                               inc=te)
                        b.step("vector", lambda E_, j=j, k=k, pst=pst, wtb=wtb:
                               E_.tensor_copy(
                                   out=wtb[:, (j * c.KD + k) * 128:(j * c.KD + k + 1) * 128],
                                   in_=pst[:, :64].bitcast(BF16)),
                               waits=[te] + (gw if j == 0 and k == 0 else []),
                               inc=ve)
                        self.pp.release(ve)
                self.wp.release(te, rng=raws_rng)
                for dst, j in ((qh, 0), (kh, 1)):
                    for o, n in nnT:
                        psq, gp = self.pp.acquire()
                        b.step("tensor", lambda E_, dst=dst, j=j, o=o, n=n, psq=psq,
                               wtb=wtb: [E_.matmul(
                                   psq[:, :n],
                                   wtb[:, (j * c.KD + k) * 128:(j * c.KD + k + 1) * 128],
                                   self.XN[:, k * T + o: k * T + o + n],
                                   start=(k == 0), stop=(k == c.KD - 1))
                                   for k in range(c.KD)][-1],
                               waits=[ve, xn_ready] + gp, inc=te)
                        b.step("scalar", lambda E_, dst=dst, o=o, n=n, psq=psq:
                               E_.activation(out=dst[:, o:o + n], in_=psq[:, :n],
                                             func=AF.Identity), waits=[te], inc=se)
                        self.pp.release(se)
                for tc4 in range(0, c.NTC, 4):
                    t4 = min(4, c.NTC - tc4)
                    psv, gp = self.pp.acquire()
                    b.step("tensor", lambda E_, tc4=tc4, t4=t4, psv=psv, wtb=wtb:
                           [E_.matmul(
                               psv[:, j * 128:(j + 1) * 128],
                               self.XN[:, k * T + (tc4 + j) * 128:
                                       k * T + (tc4 + j + 1) * 128],
                               wtb[:, (2 * c.KD + k) * 128:(2 * c.KD + k + 1) * 128],
                               start=(k == 0), stop=(k == c.KD - 1))
                               for j in range(t4) for k in range(c.KD)][-1],
                           waits=[ve, xn_ready] + gp, inc=te)
                    b.step("scalar", lambda E_, tc4=tc4, t4=t4, psv=psv: E_.activation(
                        out=vh[:, tc4 * 128:(tc4 + t4) * 128], in_=psv[:, :t4 * 128],
                        func=AF.Identity), waits=[te], inc=se)
                    self.pp.release(se)
                self.wp.release(te, rng=wtb_rng)
                nT4 = len(nnT)
                for qc in range(c.NTC):
                    qg_i = qc % c.QG
                    for o2, (o, n) in enumerate(nnT):
                        b.step("tensor", lambda E_, qc=qc, o=o, n=n: E_.matmul(
                            ps_s[:, o:o + n], qh[0:F, qc * 128:(qc + 1) * 128],
                            kh[0:F, o:o + n], start=True, stop=True),
                            waits=[se, exp_done] +
                                  (gs if h == 0 and qc == 0 else []), inc=te)
                        b.step("vector", lambda E_, o2=o2, o=o, n=n: E_.tensor_reduce(
                            out=sc[:, o2:o2 + 1], in_=ps_s[:, o:o + n], axis=AX.X,
                            op=OP.max), waits=[te], inc=ve)
                    b.step("vector", lambda E_, nT4=nT4: E_.tensor_reduce(
                        out=sc[:, 4:5], in_=sc[:, 0:nT4], axis=AX.X, op=OP.max,
                        negate=True), inc=ve)
                    b.step("vector", lambda E_: E_.tensor_scalar(
                        out=sc[:, 4:5], in0=sc[:, 4:5], scalar1=scale, scalar2=None,
                        op0=OP.mult), inc=ve)
                    b.step("scalar", lambda E_: E_.activation(
                        out=p0[:, :], in_=ps_s[:, :T], func=AF.Exp, scale=scale,
                        bias=sc[:, 4:5], accum_out=sc[:, 6:7]), waits=[ve], inc=se)
                    for o2, (o, n) in enumerate(nnT):
                        b.step("tensor", lambda E_, qc=qc, o=o, n=n: E_.matmul(
                            ps_s[:, o:o + n], qh[F:128, qc * 128:(qc + 1) * 128],
                            kh[F:128, o:o + n], start=True, stop=True),
                            waits=[se], inc=te)
                        b.step("vector", lambda E_, o2=o2, o=o, n=n: E_.tensor_reduce(
                            out=sc[:, o2:o2 + 1], in_=ps_s[:, o:o + n], axis=AX.X,
                            op=OP.max), waits=[te], inc=ve)
                    b.step("vector", lambda E_, nT4=nT4: E_.tensor_reduce(
                        out=sc[:, 5:6], in_=sc[:, 0:nT4], axis=AX.X, op=OP.max,
                        negate=True), inc=ve)
                    b.step("vector", lambda E_: E_.tensor_scalar(
                        out=sc[:, 5:6], in0=sc[:, 5:6], scalar1=scale, scalar2=None,
                        op0=OP.mult), inc=ve)
                    b.step("scalar", lambda E_: E_.activation(
                        out=att[:, :], in_=ps_s[:, :T], func=AF.Exp, scale=scale,
                        bias=sc[:, 5:6], accum_out=sc[:, 7:8]), waits=[ve], inc=se)
                    exp_done = (se, se.n)
                    b.step("vector", lambda E_: E_.reciprocal(
                        out=sc[:, 8:9], in_=sc[:, 7:8]), waits=[se], inc=ve)
                    b.step("vector", lambda E_: E_.tensor_mul(
                        sc[:, 8:9], sc[:, 8:9], sc[:, 6:7]), inc=ve)
                    b.step("vector", lambda E_: E_.tensor_mul(
                        sc[:, 8:9], sc[:, 8:9], self.dwn[:, 0:1]), inc=ve)
                    b.step("vector", lambda E_: E_.reciprocal(
                        out=sc[:, 9:10], in_=sc[:, 6:7]), inc=ve)
                    TH2 = T // 2
                    half_ve = []
                    for hv in range(2):
                        b.step("vector", lambda E_, hv=hv: E_.scalar_tensor_tensor(
                            out=att[:, hv * TH2:(hv + 1) * TH2],
                            in0=att[:, hv * TH2:(hv + 1) * TH2], scalar=sc[:, 8:9],
                            in1=p0[:, hv * TH2:(hv + 1) * TH2],
                            op0=OP.mult, op1=OP.add), inc=ve)
                        b.step("vector", lambda E_, hv=hv: E_.tensor_scalar(
                            out=att[:, hv * TH2:(hv + 1) * TH2],
                            in0=att[:, hv * TH2:(hv + 1) * TH2], scalar1=sc[:, 9:10],
                            scalar2=None, op0=OP.mult), inc=ve)
                        half_ve.append(ve.n)
                    for kc4 in range(0, c.NTC, 4):
                        k4 = min(4, c.NTC - kc4)
                        hv_need = 0 if (kc4 + k4) * 128 <= TH2 else 1
                        pst, gp = self.pp.acquire()
                        b.step("tensor", lambda E_, kc4=kc4, k4=k4, pst=pst: [
                            E_.transpose(pst[:, j * 64:(j + 1) * 64].bitcast(BF16),
                                         att[:, (kc4 + j) * 128:(kc4 + j + 1) * 128],
                                         self.idb[:, :]) for j in range(k4)][-1],
                            waits=[(ve, half_ve[hv_need])] + gp, inc=te)
                        dst = attT[:, :].rearrange(
                            "p (k q) -> p k q", q=c.QG * 128)[
                            :, kc4:kc4 + k4, qg_i * 128:(qg_i + 1) * 128]
                        b.step("vector", lambda E_, pst=pst, dst=dst, k4=k4:
                               E_.tensor_copy(out=dst, in_=pst[:, :k4 * 64]
                                              .bitcast(BF16)
                                              .rearrange("p (k q) -> p k q", k=k4)),
                               waits=[te], inc=ve)
                        self.pp.release(ve)
                    if qg_i == c.QG - 1:
                        qg0 = qc - c.QG + 1
                        pso, gp = self.pp.acquire()
                        b.step("tensor", lambda E_, pso=pso: [E_.matmul(
                            pso[:, :c.QG * 128],
                            vh[:, kc * 128:(kc + 1) * 128],
                            attT[:, kc * c.QG * 128:(kc + 1) * c.QG * 128],
                            start=(kc == 0), stop=(kc == c.NTC - 1))
                            for kc in range(c.NTC)][-1], waits=[ve] + gp, inc=te)
                        b.step("scalar", lambda E_, h=h, qg0=qg0, pso=pso:
                               E_.activation(
                                   out=o_fm[:, h * T + qg0 * 128:
                                            h * T + (qg0 + c.QG) * 128],
                                   in_=pso[:, :c.QG * 128], func=AF.Identity),
                               waits=[te], inc=se)
                        self.pp.release(se)
            self.pp.release(se, rng=rs_s)
            SH.close()
            osq = Rot([b.sb(f"osq{i}", [128, T], BF16, stack=S) for i in range(2)])
            rmsb = b.sb("rmsb", [128, T], F32, stack=S)
            ps_st, gst = self.pp.acquire(max(1, T // 512))
            for k in range(c.KD):
                sq_, gq = osq.acquire()
                b.step("scalar", lambda E_, k=k, sq_=sq_: E_.activation(
                    out=sq_[:, :], in_=o_fm[:, k * T:(k + 1) * T], func=AF.Square),
                    waits=[se] + gq, inc=se)
                b.step("tensor", lambda E_, k=k, sq_=sq_: [E_.matmul(
                    ps_st[:, o:o + n], self.onesb[:, :], sq_[:, o:o + n],
                    start=(k == 0), stop=(k == c.KD - 1)) for o, n in nnT][-1],
                    waits=[se] + (gst if k == 0 else []), inc=te)
                osq.release(te)
            b.step("scalar", lambda E_: E_.activation(
                out=rmsb[:, :], in_=ps_st[:, :T], func=AF.Ln, scale=1.0 / c.D,
                bias=EPS_RMS), waits=[te], inc=se)
            self.pp.release(se)
            b.step("scalar", lambda E_: E_.activation(
                out=rmsb[:, :], in_=rmsb[:, :], func=AF.Exp, scale=-0.5), inc=se)
            for k in range(c.KD):
                b.step("vector", lambda E_, k=k: E_.scalar_tensor_tensor(
                    out=o_fm[:, k * T:(k + 1) * T], in0=o_fm[:, k * T:(k + 1) * T],
                    scalar=self.nwc[:, k:k + 1], in1=rmsb[:, :], op0=OP.mult,
                    op1=OP.mult), waits=[se], inc=ve)
            xw_ready = (ve, ve.n)
            for d in range(c.KD):
                raws, gr, rch = self.wp.acquire()
                raws_rng = self.wp.last()
                b.dma("gpsimd", raws[:, :D], w["Wo"][d * 128:(d + 1) * 128, :],
                      waits=gr, inc=rch)
                wtb, gw0, _wc = self.wp.acquire()
                wtb_rng2 = self.wp.last()
                for k in range(c.KD):
                    pst, gp = self.pp.acquire()
                    b.step("tensor", lambda E_, k=k, pst=pst, raws=raws: E_.transpose(
                        pst[:, :64].bitcast(BF16), raws[:, k * 128:(k + 1) * 128],
                        self.idb[:, :]),
                        waits=[rch] + gp, inc=te)
                    b.step("vector", lambda E_, k=k, pst=pst, wtb=wtb: E_.tensor_copy(
                        out=wtb[:, k * 128:(k + 1) * 128],
                        in_=pst[:, :64].bitcast(BF16)),
                        waits=[te] + (gw0 if k == 0 else []), inc=ve)
                    self.pp.release(ve)
                self.wp.release(te, rng=raws_rng)
                for o, n in nnT:
                    pso, gp = self.pp.acquire()
                    b.step("tensor", lambda E_, d=d, o=o, n=n, pso=pso, wtb=wtb:
                           [E_.matmul(pso[:, :n],
                                      wtb[:, k * 128:(k + 1) * 128],
                                      o_fm[:, k * T + o: k * T + o + n],
                                      start=(k == 0), stop=(k == c.KD - 1))
                               for k in range(c.KD)][-1],
                           waits=[ve, xw_ready] + gp, inc=te)
                    b.step("scalar", lambda E_, pso=pso, d=d, o=o, n=n: E_.activation(
                        out=rmsb[:, o:o + n], in_=pso[:, :n], func=AF.Identity,
                        scale=self.sc1[:, 0:1], bias=self.boS[:, d:d + 1]),
                        waits=[te], inc=se)
                    b.step("vector", lambda E_, d=d, o=o, n=n: E_.tensor_add(
                        self.R[:, d * T + o: d * T + o + n],
                        self.R[:, d * T + o: d * T + o + n], rmsb[:, o:o + n]),
                        waits=[se], inc=ve)
                    self.pp.release(se)
                self.wp.release(te, rng=wtb_rng2)
        return (ve, ve.n)

    def l1ffn(self, r_ready):
        b, c = self.b, self.c
        te, ve, se, dg = self.te, self.ve, self.se, self.dg
        T, TH, D, H1 = c.T, c.TH, c.D, c.H1
        w = self.w
        nnD = nchunks(D)

        SB = ExitStack()
        hact = acc = None

        def _half(half, r_ready, prep, preps_done):
            off = half * TH
            nn = nchunks(TH)
            first_mm = [preps_done]
            with ExitStack() as S:
                xn_ready, cw_tm, cw_fm, w_ready = prep
                units = [("sh", w["sW1"], w["sW2"])] + [
                    ("ex", w["eW1"][e], w["eW2"][e]) for e in range(c.E1)]
                for ui, (kind, w1, w2) in enumerate(units):
                    e = ui - 1
                    upls = []
                    for mb0 in range(0, c.KH1, c.MB):
                        mbn = min(c.MB, c.KH1 - mb0)
                        slot, gw, wch = self.wp.acquire()
                        rng_u = self.wp.last()
                        b.dma("gpsimd",
                              slot[:, :c.KD * mbn * 128].rearrange(
                                  "p (k m) -> p k m", k=c.KD),
                              w1[:, mb0 * 128:(mb0 + mbn) * 128].rearrange(
                                  "(k p) m -> p k m", p=128),
                              waits=gw, inc=wch)
                        upls.append((mb0, mbn, slot, wch, rng_u))
                    w2s, gw2, w2ch = self.wp.acquire(2)
                    rw2 = self.wp.last()
                    for rc in range(0, c.KH1, 4):
                        rn = min(4, c.KH1 - rc)
                        b.dma("gpsimd",
                              w2s[:, rc * D:(rc + rn) * D].rearrange(
                                  "p (k m) -> p k m", k=rn),
                              w2[rc * 128:(rc + rn) * 128, :].rearrange(
                                  "(k p) m -> p k m", p=128),
                              waits=gw2 if rc == 0 else [], inc=w2ch)
                    for mb0, mbn, slot, wch, rng_u in upls:
                        for mm in range(mbn):
                            m = mb0 + mm
                            psu, gu = self.pp.acquire(max(1, TH // 512))
                            b.step("tensor", lambda E_, slot=slot, mm=mm, mbn=mbn,
                                   psu=psu: [E_.matmul(
                                       psu[:, o:o + n],
                                       slot[:, (k * mbn + mm) * 128:
                                            (k * mbn + mm) * 128 + 128],
                                       self.XN[:, k * T + off + o:
                                               k * T + off + o + n],
                                       start=(k == 0), stop=(k == c.KD - 1))
                                       for o, n in nn for k in range(c.KD)][-1],
                                   waits=[wch, xn_ready] + gu + first_mm, inc=te)
                            bia = (self.sb1c[:, m:m + 1] if kind == "sh"
                                   else self.eb1c[:, e * c.KH1 + m:e * c.KH1 + m + 1])
                            b.step("scalar", lambda E_, m=m, psu=psu, bia=bia:
                                   E_.activation(
                                       out=hact[:, m * TH:(m + 1) * TH],
                                       in_=psu[:, :TH], func=AF.Relu, bias=bia),
                                   waits=[te], inc=se)
                            self.pp.release(se)
                        self.wp.release(te, rng=rng_u)
                    for tc in range(c.NTCH):
                        psd, gd = self.pp.acquire(max(1, D // 512))
                        def dn_mm(E_, psd=psd, tc=tc, kind=kind, w2s=w2s):
                            last = None
                            for o, n in nnD:
                                for k in range(c.KH1):
                                    last = E_.matmul(
                                        psd[:, o:o + n],
                                        hact[:, k * TH + tc * 128:
                                             k * TH + (tc + 1) * 128],
                                        w2s[:, k * D + o: k * D + o + n],
                                        start=(k == 0),
                                        stop=(kind != "sh" and k == c.KH1 - 1))
                                if kind == "sh":
                                    E_.matmul(psd[:, o:o + n],
                                              self.onesb[0:1, :128],
                                              self.sb2r[0:1, o:o + n],
                                              start=False, stop=False)
                                    last = E_.matmul(
                                        psd[:, o:o + n],
                                        cw_fm[:, tc * 128:(tc + 1) * 128],
                                        self.eb2r[:, o:o + n],
                                        start=False, stop=True)
                            return last
                        b.step("tensor", dn_mm,
                               waits=[w2ch, se, w_ready, (self.dg, 80)] + gd,
                               inc=te)
                        if kind == "sh":
                            b.step("vector", lambda E_, tc=tc, psd=psd: E_.tensor_copy(
                                out=acc[:, tc * D:(tc + 1) * D], in_=psd[:, :D]),
                                waits=[te], inc=ve)
                        else:
                            b.step("vector", lambda E_, tc=tc, psd=psd, e=e:
                                   E_.scalar_tensor_tensor(
                                       out=acc[:, tc * D:(tc + 1) * D],
                                       in0=psd[:, :D],
                                       scalar=cw_tm[:, tc * c.E1 + e:
                                                    tc * c.E1 + e + 1],
                                       in1=acc[:, tc * D:(tc + 1) * D],
                                       op0=OP.mult, op1=OP.add), waits=[te], inc=ve)
                        self.pp.release(ve)
                    self.wp.release(te, rng=rw2)
                for tc in range(c.NTCH):
                    for g4 in range(0, c.KD, 4):
                        kk = min(4, c.KD - g4)
                        pst, gp = self.pp.acquire()
                        b.step("tensor", lambda E_, tc=tc, g4=g4, kk=kk, pst=pst:
                               [E_.transpose(
                                   pst[:, j * 128:(j + 1) * 128],
                                   acc[:, tc * D + (g4 + j) * 128:
                                       tc * D + (g4 + j + 1) * 128],
                                   self.idf[:, :]) for j in range(kk)][-1],
                               waits=[ve] + gp, inc=te)
                        rsl = self.R[:, :].rearrange("p (k t) -> p k t", k=c.KD)[
                            :, g4:g4 + kk, off + tc * 128: off + (tc + 1) * 128]
                        b.step("vector", lambda E_, pst=pst, rsl=rsl, kk=kk:
                               E_.scalar_tensor_tensor(
                                   out=rsl, in0=pst[:, :kk * 128].rearrange(
                                       "p (k t) -> p k t", k=kk),
                                   scalar=0.5, in1=rsl, op0=OP.mult, op1=OP.add),
                               waits=[te], inc=ve)
                        self.pp.release(ve)
            return (ve, ve.n)

        with ExitStack() as SP:
            preps = []
            rr = r_ready
            for half in range(2):
                xn_ready = self.ln(SP, half * TH, TH, rr)
                cw_tm, cw_fm, w_ready = self.topk_weights(
                    SP, xn_ready, half * TH, TH, 1)
                preps.append((xn_ready, cw_tm, cw_fm, w_ready))
                rr = (self.ve, self.ve.n)
            hact = b.sb("hact1", [128, c.KH1 * TH], BF16, stack=SB)
            acc = b.sb("acc1", [128, c.NTCH * D], F32, stack=SB)
            pd = (self.ve, self.ve.n)
            for half in range(2):
                r_ready = _half(half, r_ready, preps[half], pd)
            SB.close()
        return r_ready

    def l1attn(self, r_ready):
        b, c = self.b, self.c
        te, ve, se, dg = self.te, self.ve, self.se, self.dg
        T, D = c.T, c.D
        w = self.w
        nnT = nchunks(T)
        with ExitStack() as S:
            S2 = ExitStack()
            sq = Rot([b.sb(f"a1sq{i}", [128, T], BF16, stack=S2) for i in range(2)])
            xb = Rot([b.sb(f"a1xb{i}", [128, T], BF16, stack=S2) for i in range(2)])
            mb = b.sb("a1m", [128, T], F32, stack=S2)
            vr = b.sb("a1v", [128, T], F32, stack=S2)
            s1 = b.sb("a1s1", [128, T], F32, stack=S2)
            fb = b.sb("a1f", [128, T], F32, stack=S2)
            ps_sm, gsm = self.pp.acquire(max(1, T // 512))
            rs_sm = self.pp.last()
            ps_st, gst = self.pp.acquire(max(1, T // 512))
            rs_st = self.pp.last()
            for k in range(c.KD):
                a0 = k * T
                sq_, gq = sq.acquire()
                b.step("scalar", lambda E_, a0=a0, sq_=sq_: E_.activation(
                    out=sq_[:, :], in_=self.R[:, a0:a0 + T], func=AF.Square),
                    waits=[r_ready] + gq, inc=se)
                xb_, gx = xb.acquire()
                b.step("vector", lambda E_, a0=a0, xb_=xb_: E_.tensor_copy(
                    out=xb_[:, :], in_=self.R[:, a0:a0 + T]),
                    waits=[r_ready] + gx, inc=ve)
                b.step("tensor", lambda E_, xb_=xb_, k=k: [E_.matmul(
                    ps_sm[:, o:o + n], self.onesb[:, :], xb_[:, o:o + n],
                    start=(k == 0), stop=(k == c.KD - 1)) for o, n in nnT][-1],
                    waits=[ve] + (gsm if k == 0 else []), inc=te)
                xb.release(te)
                b.step("tensor", lambda E_, k=k, sq_=sq_: [E_.matmul(
                    ps_st[:, o:o + n], self.onesb[:, :], sq_[:, o:o + n],
                    start=(k == 0), stop=(k == c.KD - 1)) for o, n in nnT][-1],
                    waits=[se] + (gst if k == 0 else []), inc=te)
                sq.release(te)
            rD = 1.0 / c.D
            b.step("vector", lambda E_: E_.tensor_scalar(
                out=mb[:, :], in0=ps_sm[:, :T], scalar1=rD, scalar2=None,
                op0=OP.mult), waits=[te], inc=ve)
            self.pp.release(ve, rng=rs_sm)
            b.step("vector", lambda E_: E_.tensor_mul(vr[:, :], mb[:, :], mb[:, :]),
                   inc=ve)
            b.step("vector", lambda E_: E_.scalar_tensor_tensor(
                out=vr[:, :], in0=ps_st[:, :T], scalar=rD, in1=vr[:, :],
                op0=OP.mult, op1=OP.subtract), inc=ve)
            self.pp.release(ve, rng=rs_st)
            b.step("scalar", lambda E_: E_.activation(
                out=s1[:, :], in_=vr[:, :], func=AF.Ln, bias=1e-5),
                waits=[ve], inc=se)
            b.step("scalar", lambda E_: E_.activation(
                out=s1[:, :], in_=s1[:, :], func=AF.Exp, scale=-0.5), inc=se)
            b.step("vector", lambda E_: E_.tensor_scalar(
                out=fb[:, :], in0=vr[:, :], scalar1=float(c.D), scalar2=1e-30,
                op0=OP.mult, op1=OP.max), inc=ve)
            b.step("scalar", lambda E_: E_.activation(
                out=fb[:, :], in_=fb[:, :], func=AF.Ln), waits=[ve], inc=se)
            b.step("scalar", lambda E_: E_.activation(
                out=fb[:, :], in_=fb[:, :], func=AF.Exp, scale=0.5), inc=se)
            b.step("vector", lambda E_: E_.tensor_mul(fb[:, :], fb[:, :], s1[:, :]),
                   waits=[se], inc=ve)
            b.step("vector", lambda E_: E_.tensor_scalar(
                out=fb[:, :], in0=fb[:, :], scalar1=1e-12, scalar2=None,
                op0=OP.max), inc=ve)
            b.step("scalar", lambda E_: E_.activation(
                out=fb[:, :], in_=fb[:, :], func=AF.Ln), waits=[ve], inc=se)
            b.step("scalar", lambda E_: E_.activation(
                out=fb[:, :], in_=fb[:, :], func=AF.Exp, scale=-1.0,
                bias=self.lnD2), inc=se)
            b.step("vector", lambda E_: E_.tensor_mul(fb[:, :], fb[:, :], s1[:, :]),
                   waits=[se], inc=ve)
            for k in range(c.KD):
                a0 = k * T
                b.step("vector", lambda E_, a0=a0: E_.tensor_sub(
                    s1[:, :], self.R[:, a0:a0 + T], mb[:, :]), inc=ve)
                b.step("vector", lambda E_, k=k, a0=a0: E_.scalar_tensor_tensor(
                    out=self.XN[:, a0:a0 + T], in0=s1[:, :],
                    scalar=self.gmc[:, k:k + 1], in1=fb[:, :],
                    op0=OP.mult, op1=OP.mult), inc=ve)
            xn_ready = (ve, ve.n)
            S2.close()
            qb = b.sb("a1q", [128, T], F32, stack=S)
            kvb = b.sb("a1kv", [128, T], F32, stack=S)
            ab = b.sb("a1a", [128, T], F32, stack=S)
            snb = b.sb("a1sn", [128, T], F32, stack=S)

            for cc in range(c.KD):
                raws, gr, rch = self.wp.acquire()
                raws_rng = self.wp.last()
                for j in range(3):
                    b.dma("gpsimd", raws[:, j * D:(j + 1) * D],
                          w["W_qkva"][j * D + cc * 128: j * D + (cc + 1) * 128, :],
                          waits=gr if j == 0 else [], inc=rch)
                wtb, gw, _wc = self.wp.acquire()
                wtb_rng = self.wp.last()
                for j in range(3):
                    for k in range(c.KD):
                        pst, gp = self.pp.acquire()
                        b.step("tensor", lambda E_, j=j, k=k, pst=pst, raws=raws:
                               E_.transpose(
                                   pst[:, :64].bitcast(BF16),
                                   raws[:, j * D + k * 128: j * D + (k + 1) * 128],
                                   self.idb[:, :]),
                               waits=[rch] + gp,
                               inc=te)
                        b.step("vector", lambda E_, j=j, k=k, pst=pst, wtb=wtb:
                               E_.tensor_copy(
                                   out=wtb[:, (j * c.KD + k) * 128:
                                           (j * c.KD + k + 1) * 128],
                                   in_=pst[:, :64].bitcast(BF16)),
                               waits=[te] + (gw if j == 0 and k == 0 else []),
                               inc=ve)
                        self.pp.release(ve)
                self.wp.release(te, rng=raws_rng)
                for j, dstb in ((0, qb), (1, kvb), (2, ab)):
                    psq, gp = self.pp.acquire(max(1, T // 512))
                    b.step("tensor", lambda E_, j=j, psq=psq, wtb=wtb: [E_.matmul(
                        psq[:, o:o + n],
                        wtb[:, (j * c.KD + k) * 128:(j * c.KD + k + 1) * 128],
                        self.XN[:, k * T + o: k * T + o + n],
                        start=(k == 0), stop=(k == c.KD - 1))
                        for o, n in nnT for k in range(c.KD)][-1],
                        waits=[ve, xn_ready] + gp, inc=te)
                    if j == 2:
                        b.step("scalar", lambda E_, psq=psq: E_.activation(
                            out=ab[:, :], in_=psq[:, :T], func=AF.Sigmoid),
                            waits=[te], inc=se)
                        self.pp.release(se)
                    else:
                        b.step("vector", lambda E_, psq=psq, dstb=dstb: E_.tensor_copy(
                            out=dstb[:, :], in_=psq[:, :T]), waits=[te], inc=ve)
                        self.pp.release(ve)
                b.step("vector", lambda E_: E_.tensor_tensor_scan(
                    out=snb[:, :], data0=ab[:, :], data1=kvb[:, :], initial=0.0,
                    op0=OP.mult, op1=OP.add), waits=[se], inc=ve)
                b.step("vector", lambda E_: E_.tensor_mul(
                    snb[:, :], snb[:, :], qb[:, :]), inc=ve)
                b.step("vector", lambda E_, cc=cc: E_.tensor_add(
                    self.R[:, cc * T:(cc + 1) * T], self.R[:, cc * T:(cc + 1) * T],
                    snb[:, :]), inc=ve)
                self.wp.release(te, rng=wtb_rng)
        return (ve, ve.n)

    def load_x(self, si):
        b, c = self.b, self.c
        T, D = c.T, c.D
        xs = self.x_h[si * T:(si + 1) * T, :]
        for tc in range(c.NTC):
            buf, g, sch = self.stg.acquire()
            b.dma("sync", buf[:, :D], xs[tc * 128:(tc + 1) * 128, :],
                  waits=g, inc=sch)
            for g4 in range(0, c.KD, 4):
                kk = min(4, c.KD - g4)
                ps_, gp = self.pp.acquire()
                b.step("tensor", lambda E_, buf=buf, g4=g4, kk=kk, ps_=ps_:
                       [E_.transpose(ps_[:, j * 128:(j + 1) * 128],
                                     buf[:, (g4 + j) * 128:(g4 + j + 1) * 128],
                                     self.idf[:, :]) for j in range(kk)][-1],
                       waits=[sch, self.ge] + gp, inc=self.te)
                dst = self.R[:, :].rearrange("p (k t) -> p k t", k=c.KD)[
                    :, g4:g4 + kk, tc * 128:(tc + 1) * 128]
                b.step("vector", lambda E_, ps_=ps_, dst=dst, kk=kk: E_.tensor_copy(
                    out=dst, in_=ps_[:, :kk * 128].rearrange(
                        "p (k t) -> p k t", k=kk)), waits=[self.te], inc=self.ve)
                self.pp.release(self.ve)
            self.stg.release(self.te)
        return (self.ve, self.ve.n)

    def store_x(self, si, r_ready):
        b, c = self.b, self.c
        T, D = c.T, c.D
        xs = self.out_h[si * T:(si + 1) * T, :]
        for tc in range(c.NTC):
            buf, g, sch = self.stg.acquire()
            for g4 in range(0, c.KD, 4):
                kk = min(4, c.KD - g4)
                ps_, gp = self.pp.acquire()
                b.step("tensor", lambda E_, tc=tc, g4=g4, kk=kk, ps_=ps_:
                       [E_.transpose(
                           ps_[:, j * 128:(j + 1) * 128],
                           self.R[:, (g4 + j) * T + tc * 128:
                                  (g4 + j) * T + (tc + 1) * 128],
                           self.idf[:, :]) for j in range(kk)][-1],
                       waits=[r_ready, self.ge] + gp, inc=self.te)
                b.step("vector", lambda E_, ps_=ps_, buf=buf, g4=g4, kk=kk:
                       E_.tensor_copy(out=buf[:, g4 * 128:(g4 + kk) * 128],
                                      in_=ps_[:, :kk * 128]),
                       waits=[self.te] + (g if g4 == 0 else []), inc=self.ve)
                self.pp.release(self.ve)
            b.dma("sync", xs[tc * 128:(tc + 1) * 128, :], buf[:, :D],
                  waits=[self.ve], inc=sch)
            self.stg.release(sch)

    def build(self):
        self.declare_io()
        self.alloc()
        self.prologue()
        c = self.c
        for si in range(c.S):
            r_ready = self.load_x(si)
            for sub in c.sub:
                if sub == "f0":
                    r_ready = self.l0ffn(r_ready)
                elif sub == "a0":
                    r_ready = self.l0attn(r_ready)
                elif sub == "f1":
                    r_ready = self.l1ffn(r_ready)
                elif sub == "a1":
                    r_ready = self.l1attn(r_ready)
            self.store_x(si, r_ready)
        self.b.step("sync", lambda E: E.nop(),
                    waits=[(ch, ch.n) for ch in self.stg.chans])
        self.b.finalize()
        return self.nc


def kernel(**inputs):
    cfg = CFG()
    kb = KB(cfg)
    nc = kb.build()
    x = np.ascontiguousarray(inputs["L0_attn_x"], dtype=np.float32)
    B_, T_, D_ = x.shape
    per = B_ // cfg.n_cores
    in_maps = []
    for ci in range(cfg.n_cores):
        m = {"x": x[ci * per:(ci + 1) * per].reshape(per * T_, D_)}
        for k, v in inputs.items():
            if k == "L0_attn_x":
                continue
            a = np.ascontiguousarray(v, dtype=np.float32)
            if k == "L0_attn_diff_weight":
                a = a.reshape(1, 1)
            if k == "L0_attn_norm_w":
                a = a.reshape(-1)
            m[k] = a
        in_maps.append(m)
    res = run_bass_kernel_spmd(nc, in_maps, core_ids=list(range(cfg.n_cores)))
    out = np.concatenate(
        [res.results[ci]["out"].reshape(per, T_, D_) for ci in range(cfg.n_cores)],
        axis=0)
    return out
